# version 20
# baseline (speedup 1.0000x reference)
import numpy as np
import ml_dtypes
from contextlib import ExitStack

import concourse.bass as bass
import concourse.mybir as mybir
from concourse.bass_utils import run_bass_kernel_spmd

F32 = mybir.dt.float32
BF16 = mybir.dt.bfloat16
FP8 = mybir.dt.float8e4
U8 = mybir.dt.uint8
AF = mybir.ActivationFunctionType
ALU = mybir.AluOpType
AX = mybir.AxisListType

MAGIC = float(np.float32(1.5 * 2.0 ** 23))
H, DK, RNK, N, D, HID = 16, 64, 32, 2048, 1024, 4096
NT = 16
EPS_LN, EPS_RMS, EPS_Q = 1e-5, 1e-8, 1e-5
KB = 1024

ENGINES = ("tensor", "vector", "scalar", "sync", "gpsimd")


def sigma3_perm():
    perm = np.zeros(D, dtype=np.int64)
    for g in range(4):
        for i in range(4):
            h = 4 * g + i
            for u in range(32):
                perm[256 * g + 32 * i + u] = 64 * h + 2 * u
                perm[256 * g + 128 + 32 * i + u] = 64 * h + 2 * u + 1
    return perm


def y_perm():
    yp = np.zeros(D, dtype=np.int64)
    for h in range(H):
        for u in range(32):
            yp[64 * h + u] = 64 * h + 2 * u
            yp[64 * h + 32 + u] = 64 * h + 2 * u + 1
    return yp


def rotary_tables():
    half = DK // 2
    inv = 1.0 / 10000.0 ** (np.arange(half, dtype=np.float64) / half)
    ang = np.arange(N, dtype=np.float64)[:, None] * inv[None, :]
    cos = np.cos(ang).astype(np.float32)
    sin = np.sin(ang).astype(np.float32)
    mt = np.zeros(N, dtype=np.int64)
    for r in range(16):
        for q in range(128):
            mt[128 * r + q] = 16 * q + r
    cosk, sink = cos[mt], sin[mt]

    def tok(a):
        return np.ascontiguousarray(
            a.reshape(NT, 128, 32).transpose(1, 0, 2)).astype(ml_dtypes.bfloat16)

    return tok(cos), tok(sin), tok(cosk), tok(sink)


def sE(r):
    return 256 * (r // 4) + 32 * (r % 4)


class Sem:
    def __init__(self, handle, name):
        self.h = handle
        self.name = name
        self.count = 0


class Step:
    __slots__ = ("fn", "waits", "inc_sem", "inc_amt", "token", "needed")

    def __init__(self, fn, waits, inc_sem, inc_amt):
        self.fn = fn
        self.waits = waits
        self.inc_sem = inc_sem
        self.inc_amt = inc_amt
        self.token = None
        self.needed = False


class Plan:
    def __init__(self, nc, stack):
        self.steps = {e: [] for e in ENGINES}
        self.sems = {}
        for name in ("sT", "sV", "sS", "dX", "dWa", "dWb", "dR",
                     "dMa", "dMb", "dG", "dOUT"):
            self.sems[name] = Sem(stack.enter_context(nc.semaphore(name)), name)
        self.eng_sem = {"tensor": self.sems["sT"], "vector": self.sems["sV"],
                        "scalar": self.sems["sS"], "sync": self.sems["dX"],
                        "gpsimd": self.sems["dG"]}

    def step(self, eng, fn, waits=(), dma_out=False, sem=None, noinc=False):
        if sem is not None:
            sem = self.sems[sem]
        else:
            sem = self.sems["dOUT"] if dma_out else self.eng_sem[eng]
        amt = 16 if eng in ("sync", "gpsimd") or dma_out else 1
        w = []
        for t in waits:
            if t is None:
                continue
            s, v = t
            assert v <= s.count, f"wait on future of {s.name}: {v} > {s.count}"
            w.append((s, v))
        st = Step(fn, w, sem, amt)
        if noinc:
            st.inc_amt = 0
        else:
            sem.count += amt
        st.token = (sem, sem.count)
        self.steps[eng].append(st)
        return st.token

    def mark_needed(self):
        maxw = {}
        for e in ENGINES:
            for st in self.steps[e]:
                for (s, v) in st.waits:
                    maxw[s.name] = max(maxw.get(s.name, 0), v)
        for e in ENGINES:
            for st in self.steps[e]:
                st.needed = st.inc_amt > 0

    def replay(self, eng, engobj):
        for st in self.steps[eng]:
            for (s, v) in st.waits:
                engobj.wait_ge(s.h, v)
            inst = st.fn(engobj)
            if st.needed:
                assert inst is not None, f"needed step on {eng} returned None"
                inst.then_inc(st.inc_sem.h, st.inc_amt)


def build_program(debug=False, stop_phase=None):
    nc = bass.Bass("TRN2", target_bir_lowering=False)
    stack = ExitStack()

    def din(name, shape, dtype=F32):
        return nc.dram_tensor(name, list(shape), dtype,
                              kind="ExternalInput").ap()

    x_in = din("x", [N, D])
    c_in = din("c", [1, D])
    wqkvT = din("wqkvT", [D, D])
    bq_in = din("bq", [1, D])
    woT_in = din("woT", [D, D])
    w1T_in = din("w1T", [D, HID])
    w2T_in = din("w2T", [HID, D])
    u1rep_in = din("u1rep", [128, RNK])
    u2rep_in = din("u2rep", [128, RNK])
    wmod_in = din("wmod", [D, 6 * D])
    n1w_in = din("n1w", [1, D])
    n2w_in = din("n2w", [1, HID])
    asc_in = din("asc", [1, 1])
    abi_in = din("abi", [1, 1])
    idf_in = din("idf", [128, 128], F32)
    idb_in = din("idb", [128, 128], BF16)
    ctq_in = din("ctq", [128, NT, 32], BF16)
    stq_in = din("stq", [128, NT, 32], BF16)
    ctk_in = din("ctk", [128, NT, 32], BF16)
    stk_in = din("stk", [128, NT, 32], BF16)
    onesrow_in = din("onesrow", [1, 128], F32)
    onescol_in = din("onescol", [128, 1], BF16)

    out_hbm = nc.dram_tensor("out", [N, D], F32, kind="ExternalOutput").ap()
    modstash = nc.dram_tensor("modstash", [1, 6 * D], F32, kind="Internal").ap()
    DBG = {}
    if debug:
        DBG["proj"] = nc.dram_tensor("d_proj", [128, NT * D], BF16,
                                     kind="ExternalOutput").ap()
        DBG["ql0"] = nc.dram_tensor("d_ql0", [128, N], BF16,
                                    kind="ExternalOutput").ap()
        DBG["kl0"] = nc.dram_tensor("d_kl0", [128, N], BF16,
                                    kind="ExternalOutput").ap()
        DBG["cols"] = nc.dram_tensor("d_cols", [128, 96], F32,
                                     kind="ExternalOutput").ap()
        DBG["ytok"] = nc.dram_tensor("d_ytok", [128, 4 * H * 64], F32,
                                     kind="ExternalOutput").ap()
        DBG["x2"] = nc.dram_tensor("d_x2", [128, NT * D], F32,
                                   kind="ExternalOutput").ap()
        DBG["r2t0"] = nc.dram_tensor("d_r2t0", [128, HID], BF16,
                                     kind="ExternalOutput").ap()
        DBG["ro1"] = nc.dram_tensor("d_ro1", [128, NT * 128], BF16,
                                    kind="ExternalOutput").ap()
        DBG["ts1"] = nc.dram_tensor("d_ts1", [128, N], BF16,
                                    kind="ExternalOutput").ap()
        DBG["mod"] = nc.dram_tensor("d_mod", [1, 6 * D], F32,
                                    kind="ExternalOutput").ap()

    def sb(name, shape, dtype):
        return stack.enter_context(nc.sbuf_tensor(name, list(shape), dtype)).ap()

    xt = sb("xt", [128, NT, D], F32)
    idf = sb("idf_t", [128, 128], F32)
    idb = sb("idb_t", [128, 128], BF16)
    onesrow = sb("onesrow_t", [1, 128], F32)
    onescol = sb("onescol_t", [128, 1], BF16)
    u1b = sb("u1b", [128, RNK], BF16)
    u2b = sb("u2b", [128, RNK], BF16)
    u1qb = sb("u1qb", [128, RNK], BF16)
    u2qb = sb("u2qb", [128, RNK], BF16)
    u1f = sb("u1f", [128, RNK], F32)
    u2f = sb("u2f", [128, RNK], F32)
    siluc = sb("siluc", [128, 8], F32)
    silucb = sb("silucb", [128, 8], BF16)
    cols = sb("cols", [128, 96], F32)
    cells = sb("cells", [1, 64], F32)
    scrA = sb("scrA", [128, D], F32)
    scrB = sb("scrB", [128, D], F32)
    q4T = sb("q4T", [128, 3, 8, 128], BF16)
    rtok = sb("rtok", [128, D], BF16)
    r2Tm = sb("r2Tm", [128, 2, 4, 128], BF16)
    ARENA_KB = 123
    arena = sb("arena", [128, ARENA_KB * KB], U8)

    def reg(off_kb, shape, dtype):
        n = int(np.prod(shape[1:]))
        nbytes = n * mybir.dt.size(dtype)
        assert off_kb * KB + nbytes <= ARENA_KB * KB, f"arena overflow @{off_kb}"
        v = arena[:, off_kb * KB: off_kb * KB + nbytes].bitcast(dtype)
        if len(shape) == 3:
            v = v.rearrange("p (a b) -> p a b", a=shape[1])
        elif len(shape) == 4:
            v = v.rearrange("p (a b c) -> p a b c", a=shape[1], b=shape[2])
        return v

    wmodb = reg(0, [128, 2, 2 * D], BF16)
    modrow = reg(24, [128, 6 * D], F32)[0:1]
    silu_t = reg(48, [128, 2, D], F32)[0:1]
    gsel = reg(56, [128, 3 * D], F32)[0:1]
    crow = reg(84, [128, D], F32)[0:1]
    wstr2 = reg(8, [128, 2, D], F32)
    wabs2 = reg(16, [128, D], BF16)
    arow1 = reg(18, [128, D], F32)[0:1]
    arow2 = reg(52, [128, D], F32)[0:1]
    projbf = reg(0, [128, NT, D], BF16)
    ca1rep = reg(68, [128, D], F32)
    sarep = reg(72, [128, D], F32)
    aqrep = reg(76, [128, D], F32)
    bqrep = reg(80, [128, D], F32)
    sgq = reg(88, [128, 8, D], FP8)
    qlS = [reg(33 + 4 * g, [128, N], BF16) for g in range(4)]
    klS = [reg(49 + 4 * g, [128, N], BF16) for g in range(4)]
    ket = reg(65, [128, NT, 128], BF16)
    kot = reg(69, [128, NT, 128], BF16)
    ro1 = reg(73, [128, NT, 128], BF16)
    ro2 = reg(77, [128, NT, 128], BF16)
    ts1 = reg(81, [128, N], BF16)
    ts2 = reg(85, [128, N], BF16)
    scrC = reg(89, [128, N], BF16)
    ctq = reg(110, [128, NT, 32], BF16)
    stq = reg(111, [128, NT, 32], BF16)
    ctk = reg(112, [128, NT, 32], BF16)
    stk = reg(113, [128, NT, 32], BF16)
    sgo = reg(98, [128, 8, D], FP8)
    wstrP = reg(110, [128, 2, 1024], F32)
    garep = reg(106, [128, D], F32)
    ctabs_hi = 110
    vscr = reg(65, [128, NT, H, 65], BF16)
    ytok = reg(0, [128, 4, H, 64], F32)
    ystg = reg(16, [128, 4, 512], F32)[0:65]
    pbuf = reg(24, [128, 2, N], BF16)
    n1wrow = reg(0, [128, D], F32)[0:1]
    n2wrow = reg(4, [128, HID], F32)[0:1]
    modrow2 = reg(20, [128, 6 * D], F32)[0:1]
    cb1rep = reg(48, [128, D], BF16)
    sbrep = reg(50, [128, D], BF16)
    gbrep = reg(52, [128, D], BF16)
    w1nrep = reg(54, [128, D], F32)
    w2nrep = reg(58, [128, HID], BF16)
    wstrW = reg(0, [128, 2, 2048], F32)
    w2q = reg(16, [128, 32, D], FP8)
    wq_scr = reg(66, [128, 2048], F32)
    wabsW = reg(74, [128, 2048], BF16)
    w1q = reg(82, [128, 8, HID], FP8)
    fbufA = reg(0, [128, 8, 512], F32)
    rbuf = reg(66, [128, 2, HID], BF16)
    r2tok = reg(114, [128, HID], BF16)

    psumA = stack.enter_context(nc.psum_tensor("psumA", [128, N], F32)).ap()
    psumB = stack.enter_context(nc.psum_tensor("psumB", [128, N], F32)).ap()
    psumAb = psumA.bitcast(BF16)
    psumBb = psumB.bitcast(BF16)

    P = Plan(nc, stack)
    sp = P.step

    COLNAMES = ["sx", "ssq", "mu", "mu2", "var", "lnt", "rstd", "bias",
                "ma", "rma", "v1", "lnt2", "rs1", "mx", "mxr", "s127",
                "sc", "cn", "v2", "lnt3", "rs2", "mx2", "mx2r", "s127b",
                "sc2", "cn2", "asc", "abi", "thr", "s1g", "s2g", "wsum", "wsumt",
                "gneg0", "gneg1", "gneg2", "gneg3", "sqv"]
    COL = {nme: cols[:, i:i + 1] for i, nme in enumerate(COLNAMES)}
    maq16 = [cols[:, 40 + t:41 + t] for t in range(NT)]
    may4 = [cols[:, 56 + s:57 + s] for s in range(4)]
    rdcols = cols[:, 60:64]
    C1 = lambda i: cells[0:1, i:i + 1]

    tok_x = sp("sync", lambda e: e.dma_start(
        out=xt[:], in_=x_in.rearrange("(t p) d -> p t d", p=128)))
    tok_consts = None
    for dst, src in ((idf, idf_in), (idb, idb_in), (ctq, ctq_in),
                     (stq, stq_in), (ctk, ctk_in), (stk, stk_in),
                     (onesrow, onesrow_in), (onescol, onescol_in),
                     (u1f, u1rep_in), (u2f, u2rep_in), (crow, c_in),
                     (cells[0:1, 16:17], asc_in), (cells[0:1, 17:18], abi_in)):
        tok_consts = sp("sync", lambda e, d=dst, s=src: e.dma_start(out=d, in_=s))

    tok_S0 = (P.sems["dX"], P.sems["dX"].count)
    tok_x = tok_consts = tok_S0

    tok_cols0 = sp("vector", lambda e: e.memset(cols[:], 0.0))
    tok_ucast = tok_cols0
    for dst, src, scl in ((u1b, u1f, 1.0), (u2b, u2f, 1.0),
                          (u1qb, u1f, 0.125), (u2qb, u2f, 0.125)):
        tok_ucast = sp("vector", lambda e, d=dst, s=src, sc=scl:
                       e.tensor_scalar(out=d, in0=s, scalar1=sc, scalar2=0.0,
                                       op0=ALU.mult, op1=ALU.add),
                       waits=[tok_consts])

    rep_state = {"prev_ev": None}

    def replicate(dst, src_row, waits):
        nb = dst.shape[-1] // 512
        last = None
        for b in range(nb):
            ps = psumB[:, 512 * (b % 2):512 * (b % 2) + 512]
            w = list(waits) + ([rep_state["prev_ev"]]
                               if rep_state["prev_ev"] else [])
            tmm = sp("tensor", lambda e, ps=ps, b=b, s=src_row: e.matmul(
                ps, onesrow, s[0:1, 512 * b:512 * b + 512],
                start=True, stop=True), waits=w if b == 0 else [last])
            last = sp("vector", lambda e, ps=ps, b=b, d=dst: e.tensor_copy(
                out=d[:, 512 * b:512 * b + 512], in_=ps), waits=[tmm])
            rep_state["prev_ev"] = last
        return last

    def replicate_col(dst_col, src11, waits):
        w = list(waits) + ([rep_state["prev_ev"]]
                           if rep_state["prev_ev"] else [])
        tmm = sp("tensor", lambda e, s=src11: e.matmul(
            psumB[:, 1536:1537], onesrow, s, start=True, stop=True), waits=w)
        last = sp("vector", lambda e, d=dst_col: e.tensor_copy(
            out=d, in_=psumB[:, 1536:1537]), waits=[tmm])
        rep_state["prev_ev"] = last
        return last

    tok_asc = replicate_col(COL["asc"], cells[0:1, 16:17], [tok_consts])
    tok_abi = replicate_col(COL["abi"], cells[0:1, 17:18], [tok_consts])

    t1 = sp("scalar", lambda e: e.activation(out=silu_t[0:1, 0, :], in_=crow,
                                             func=AF.Exp, scale=-1.0),
            waits=[tok_consts])
    t2 = sp("vector", lambda e: e.tensor_scalar(
        out=silu_t[0:1, 0, :], in0=silu_t[0:1, 0, :], scalar1=1.0,
        scalar2=None, op0=ALU.add), waits=[t1])
    t3 = sp("vector", lambda e: e.reciprocal(out=silu_t[0:1, 1, :],
                                             in_=silu_t[0:1, 0, :]),
            waits=[t2])
    tok_silu = sp("vector", lambda e: e.tensor_mul(
        silu_t[0:1, 0, :], crow, silu_t[0:1, 1, :]), waits=[t3])
    tp8 = []
    for k in range(8):
        tp8.append(sp("tensor", lambda e, k=k: e.transpose(
            psumA[:, k:k + 1], silu_t[0:1, 0, 128 * k:128 * k + 128],
            idf[0:1, 0:1]), waits=[tok_silu] if k == 0 else []))
    tok_sc1 = sp("vector", lambda e: e.tensor_copy(out=siluc,
                                                   in_=psumA[:, 0:8]),
                 waits=[tp8[-1]])
    tok_sc2 = sp("vector", lambda e: e.tensor_copy(out=silucb, in_=siluc),
                 waits=[tok_sc1])

    def mod_ps(j):
        if j < 2:
            return psumA[0:1, 512 * j:512 * j + 512]
        return psumB[0:1, 512 * (j - 2):512 * (j - 2) + 512]

    def prep_ternary_steps(w_hbm, sign_dst, alpha_row, tok_prev):
        st = {"mm_last": None, "v_last": tok_prev, "v_hist": [],
              "sg_last": None, "sg_hist": [], "thr": None}

        def pass1_chunk(kc):
            wdep = [st["v_last"]] if kc < 2 else [st["v_hist"][kc - 2]]
            dtok = sp("sync", lambda e, kc=kc: e.dma_start(
                out=wstr2[:, kc % 2], in_=w_hbm[128 * kc:128 * kc + 128, :]),
                waits=wdep, sem="dWa" if kc % 2 == 0 else "dWb")
            vn = sp("vector", lambda e, kc=kc: e.tensor_scalar(
                out=scrA, in0=wstr2[:, kc % 2], scalar1=-1.0, scalar2=None,
                op0=ALU.mult), waits=[dtok, st["v_last"]])
            v_last = sp("vector", lambda e, kc=kc: e.tensor_max(
                wabs2, scrA, wstr2[:, kc % 2]),
                waits=[vn] + ([st["mm_last"]] if st["mm_last"] else []))
            st["v_last"] = v_last
            st["v_hist"].append(v_last)
            for b2 in range(2):
                st["mm_last"] = sp("tensor", lambda e, kc=kc, b2=b2: e.matmul(
                    psumB[0:1, 1024 + 512 * b2:1024 + 512 * b2 + 512],
                    onescol, wabs2[:, 512 * b2:512 * b2 + 512],
                    start=(kc == 0), stop=(kc == 7)),
                    waits=[v_last] if b2 == 0 else [])

        def mid():
            e1 = sp("vector", lambda e: e.tensor_scalar(
                out=alpha_row, in0=psumB[0:1, 1024:1024 + D],
                scalar1=1.0 / D, scalar2=None, op0=ALU.mult),
                waits=[st["mm_last"]])
            e2 = sp("vector", lambda e: e.reduce_sum(
                out=C1(0), in_=psumB[0:1, 1024:1024 + D], axis=AX.X),
                waits=[st["mm_last"]])
            e3 = sp("vector", lambda e: e.tensor_scalar(
                out=C1(1), in0=C1(0), scalar1=float(0.05 / (D * D)),
                scalar2=None, op0=ALU.mult), waits=[e2])
            st["thr"] = replicate_col(COL["thr"], C1(1), [e3])
            st["sg_last"] = e1

        def pass2_chunk(kc):
            wdep = [st["thr"]] if kc < 2 else [st["sg_hist"][kc - 2]]
            dtok = sp("sync", lambda e, kc=kc: e.dma_start(
                out=wstr2[:, kc % 2], in_=w_hbm[128 * kc:128 * kc + 128, :]),
                waits=wdep, sem="dWa" if kc % 2 == 0 else "dWb")
            g1 = sp("vector", lambda e, kc=kc: e.tensor_scalar(
                out=scrA, in0=wstr2[:, kc % 2], scalar1=COL["thr"],
                scalar2=None, op0=ALU.is_ge),
                waits=[dtok, st["thr"], st["sg_last"]])
            g2 = sp("vector", lambda e, kc=kc: e.tensor_scalar(
                out=scrB, in0=wstr2[:, kc % 2], scalar1=-1.0, scalar2=None,
                op0=ALU.mult), waits=[dtok])
            g3 = sp("vector", lambda e: e.tensor_scalar(
                out=scrB, in0=scrB, scalar1=COL["thr"], scalar2=None,
                op0=ALU.is_ge), waits=[g2])
            sg_last = sp("vector", lambda e, kc=kc: e.tensor_sub(
                sign_dst[:, kc, :], scrA, scrB), waits=[g1, g3])
            st["sg_last"] = sg_last
            st["sg_hist"].append(sg_last)

        return pass1_chunk, mid, pass2_chunk, st

    p1c, pmid, p2c, prep_st = prep_ternary_steps(wqkvT, sgq, arow1, tok_abi)
    tok_mod = None
    mm_kc = []
    mm_prev_round = None
    for rnd, (c0, nob) in enumerate(((0, 4), (2048, 4), (4096, 4))):
        wm_toks, mm_kc = [], []
        ev_prev_round = tok_mod
        for kc in range(8):
            wdep = [] if (rnd == 0 and kc < 2) else (
                [mm_kc[kc - 2]] if kc >= 2 else [mm_prev_round])
            wm_toks.append(sp("gpsimd", lambda e, kc=kc, c0=c0, nob=nob:
                              e.dma_start(
                out=wmodb[:, kc % 2, 0:512 * nob],
                in_=wmod_in[128 * kc:128 * kc + 128, c0:c0 + 512 * nob]),
                waits=wdep, sem="dMa" if kc % 2 == 0 else "dMb"))
            last = None
            for j in range(nob):
                w = []
                if j == 0:
                    w = [wm_toks[kc], tok_sc2, tp8[-1]]
                    if kc == 0 and ev_prev_round is not None:
                        w.append(ev_prev_round)
                last = sp("tensor", lambda e, kc=kc, j=j: e.matmul(
                    mod_ps(j), silucb[:, kc:kc + 1],
                    wmodb[:, kc % 2, 512 * j:512 * j + 512],
                    start=(kc == 0), stop=(kc == 7)), waits=w)
            mm_kc.append(last)
            if rnd == 0:
                p1c(kc)
            elif rnd == 1:
                p2c(kc)
        mm_prev_round = mm_kc[-1]
        for j in range(nob):
            tok_mod = sp("vector", lambda e, j=j, c0=c0: e.tensor_copy(
                out=modrow[0:1, c0 + 512 * j:c0 + 512 * j + 512],
                in_=mod_ps(j)), waits=[mm_kc[-1]])
        if rnd == 0:
            pmid()
    tok_sgq = prep_st["sg_last"]

    def gate(i):
        return modrow[0:1, D * i:D * (i + 1)]

    tok_gates = tok_mod
    for gi in (2, 5):
        a = sp("vector", lambda e, gi=gi: e.tensor_scalar(
            out=gsel[0:1, 0:D], in0=gate(gi), scalar1=0.0, scalar2=None,
            op0=ALU.is_ge), waits=[tok_gates])
        a2 = sp("vector", lambda e: e.tensor_scalar(
            out=gsel[0:1, 2 * D:3 * D], in0=gsel[0:1, 0:D], scalar1=2.0,
            scalar2=-1.0, op0=ALU.mult, op1=ALU.add), waits=[a])
        m0 = sp("vector", lambda e, gi=gi: e.tensor_scalar(
            out=gsel[0:1, D:2 * D], in0=gate(gi), scalar1=-1.0,
            scalar2=None, op0=ALU.mult), waits=[tok_gates])
        m1_ = sp("vector", lambda e, gi=gi: e.tensor_max(
            gsel[0:1, D:2 * D], gsel[0:1, D:2 * D], gate(gi)), waits=[m0])
        m = sp("vector", lambda e: e.tensor_scalar(
            out=gsel[0:1, D:2 * D], in0=gsel[0:1, D:2 * D], scalar1=0.25,
            scalar2=None, op0=ALU.max), waits=[m1_])
        tok_gates = sp("vector", lambda e, gi=gi: e.tensor_mul(
            gate(gi), gsel[0:1, 2 * D:3 * D], gsel[0:1, D:2 * D]),
            waits=[a2, m])
    for gi in (1, 4):
        tok_gates = sp("vector", lambda e, gi=gi: e.tensor_scalar(
            out=gate(gi), in0=gate(gi), scalar1=1.0, scalar2=None,
            op0=ALU.add), waits=[tok_gates])
    tok_stash = sp("gpsimd", lambda e: e.dma_start(out=modstash, in_=modrow),
                   waits=[tok_gates])

    tok_ca1 = replicate(ca1rep, gate(1), [tok_gates])
    tok_sa = replicate(sarep, gate(0), [tok_gates])
    tok_aqrep = replicate(aqrep, arow1, [tok_sgq])
    tok_bq0 = sp("sync", lambda e: e.dma_start(out=crow, in_=bq_in),
                 waits=[tok_silu], sem="dR")
    tok_bqrep = replicate(bqrep, crow, [tok_bq0])
    p1c2, pmid2, p2c2, prep_st2 = prep_ternary_steps(
        woT_in, sgo, arow2, tok_aqrep)
    for kc in range(8):
        p1c2(kc)
    pmid2()
    for kc in range(8):
        p2c2(kc)
    tok_sgo = prep_st2["sg_last"]
    if stop_phase == "S2":
        return _finish_program(nc, stack, P)

    def ln_stats(xtile, w0):
        r1 = sp("vector", lambda e: e.reduce_sum(out=COL["sx"], in_=xtile,
                                                 axis=AX.X), waits=w0)
        s1 = sp("scalar", lambda e: e.activation(out=scrA, in_=xtile,
                                                 func=AF.Square,
                                                 accum_out=COL["ssq"]),
                waits=w0)
        c1 = sp("vector", lambda e: e.tensor_scalar(
            out=COL["mu"], in0=COL["sx"], scalar1=1.0 / D, scalar2=None,
            op0=ALU.mult), waits=[r1])
        c2 = sp("vector", lambda e: e.tensor_mul(COL["mu2"], COL["mu"],
                                                 COL["mu"]), waits=[c1])
        c3 = sp("vector", lambda e: e.scalar_tensor_tensor(
            out=COL["var"], in0=COL["ssq"], scalar=1.0 / D, in1=COL["mu2"],
            op0=ALU.mult, op1=ALU.subtract), waits=[s1, c2])
        c4 = sp("vector", lambda e: e.tensor_scalar(
            out=COL["var"], in0=COL["var"], scalar1=EPS_LN, scalar2=None,
            op0=ALU.add), waits=[c3])
        c5 = sp("scalar", lambda e: e.activation(out=COL["lnt"],
                                                 in_=COL["var"], func=AF.Ln),
                waits=[c4])
        c6 = sp("scalar", lambda e: e.activation(out=COL["rstd"],
                                                 in_=COL["lnt"], func=AF.Exp,
                                                 scale=-0.5), waits=[c5])
        xh = sp("vector", lambda e: e.tensor_scalar(
            out=scrA, in0=xtile, scalar1=COL["mu"], scalar2=COL["rstd"],
            op0=ALU.subtract, op1=ALU.mult), waits=[c6])
        return xh, s1

    tok_proj = [None] * NT
    q5_hist = {}
    mm1_hist = {}
    ep1_hist = {}
    prev_ev = None
    pend_ep = None
    for t in range(NT):
        w0 = [tok_x, q5_hist[t - 1] if t >= 1 else tok_sgo]
        xh, sq1 = ln_stats(xt[:, t, :], w0)
        h1a = sp("vector", lambda e: e.tensor_mul(scrA, scrA, ca1rep),
                 waits=[xh, tok_ca1])
        h1b = sp("vector", lambda e: e.tensor_add(scrA, scrA, sarep),
                 waits=[h1a, tok_sa])
        q1 = sp("vector", lambda e: e.reduce_max(
            out=COL["ma"], in_=scrA, axis=AX.X, apply_absolute_value=True),
            waits=[h1b])
        q2 = sp("vector", lambda e: e.tensor_scalar(
            out=COL["ma"], in0=COL["ma"], scalar1=1e-6, scalar2=None,
            op0=ALU.max), waits=[q1])
        q3 = sp("vector", lambda e: e.reciprocal(out=COL["rma"],
                                                 in_=COL["ma"]), waits=[q2])
        q4 = sp("vector", lambda e, t=t: e.tensor_scalar(
            out=maq16[t], in0=COL["ma"], scalar1=float(1.0 / 7.0),
            scalar2=None, op0=ALU.mult), waits=[q2])
        q5 = sp("vector", lambda e: e.tensor_scalar(
            out=scrB, in0=scrA, scalar1=COL["rma"], scalar2=7.0,
            op0=ALU.mult, op1=ALU.mult), waits=[q3, h1b])
        q5_hist[t] = q5
        q6 = sp("vector", lambda e: e.tensor_scalar(
            out=rtok, in0=scrB, scalar1=MAGIC, scalar2=MAGIC,
            op0=ALU.add, op1=ALU.subtract), waits=[q5])
        tp = []
        for fc in range(8):
            w = [q6]
            if fc == 0:
                w.append(prev_ev if prev_ev else mm_kc[-1])
                if t >= 2:
                    w.append(mm1_hist[t - 2])
            tp.append(sp("tensor", lambda e, fc=fc: e.transpose(
                psumAb[:, 128 * fc:128 * fc + 128],
                rtok[:, 128 * fc:128 * fc + 128], idb), waits=w))
        ev = []
        for half in range(2):
            ev.append(sp("vector", lambda e, half=half, t=t: e.tensor_copy(
                out=q4T.rearrange("p b k c -> p b (k c)")[
                    :, t % 2, 512 * half:512 * half + 512],
                in_=psumAb[:, 512 * half:512 * half + 512]), waits=[tp[-1]]))
        prev_ev = ev[-1]
        ps1 = psumA[:, 1024:2048] if t % 2 else psumB[:, 0:1024]
        mm = None
        for kc in range(8):
            for half in range(2):
                w = []
                if kc == 0 and half == 0:
                    w = [ev[0], ev[1], tok_sgq,
                         ep1_hist[t - 2] if t >= 2 else tok_bqrep]
                    if t == 12:
                        w.append(tok_stash)
                mm = sp("tensor", lambda e, kc=kc, half=half, t=t,
                        ps1=ps1: e.matmul(
                    ps1[:, 512 * half:512 * half + 512],
                    q4T[:, t % 2, kc, :],
                    sgq[:, kc, 512 * half:512 * half + 512],
                    start=(kc == 0), stop=(kc == 7)), waits=w)
        mm1_hist[t] = mm

        def emit_ep(t=t, mm=mm, q4=q4, ps1=ps1):
            ep = []
            for half in range(2):
                e1 = sp("vector", lambda e, half=half, t=t, ps1=ps1:
                        e.scalar_tensor_tensor(
                    out=scrB[:, 512 * half:512 * half + 512],
                    in0=ps1[:, 512 * half:512 * half + 512],
                    scalar=maq16[t],
                    in1=aqrep[:, 512 * half:512 * half + 512],
                    op0=ALU.mult, op1=ALU.mult),
                    waits=[mm, q4, tok_aqrep])
                ep.append(sp("vector", lambda e, half=half, t=t:
                             e.tensor_add(
                    projbf[:, t, 512 * half:512 * half + 512],
                    scrB[:, 512 * half:512 * half + 512],
                    bqrep[:, 512 * half:512 * half + 512]),
                    waits=[e1, tok_bqrep]))
            ep1_hist[t] = ep[-1]
            tok_proj[t] = ep[-1]

        if pend_ep is not None:
            pend_ep()
        pend_ep = emit_ep
    pend_ep()

    prev_mm = mm1_hist[NT - 1]
    t_ga = sp("vector", lambda e: e.tensor_mul(gsel[0:1, 0:D], gate(2),
                                               arow2),
              waits=[tok_sgo, tok_gates, tok_stash])
    tok_garep = replicate(garep, gsel[0:1, 0:D], [t_ga, tok_proj[NT - 1]])
    tok_proj_all = tok_proj[-1]
    if stop_phase == "P1":
        return _finish_program(nc, stack, P)
    dbg_guard = []
    if debug:
        dbg_guard.append(sp("gpsimd", lambda e: e.dma_start(
            out=DBG["proj"], in_=projbf.rearrange("p a b -> p (a b)")),
            waits=[tok_proj_all], dma_out=True))

    tok_qlkl = [None] * 4
    tok_gcol = [None] * 4
    prev_rot_v = tok_proj_all
    prev_rot_mm = prev_mm
    for g in range(4):
        qEv = projbf.rearrange("p t (gg c) -> p t gg c", gg=4)[
            :, :, g, 0:128].rearrange("p t (i u) -> p t i u", i=4)
        qOv = projbf.rearrange("p t (gg c) -> p t gg c", gg=4)[
            :, :, g, 128:256].rearrange("p t (i u) -> p t i u", i=4)
        br = lambda tab: tab[:, :, None, :].to_broadcast((128, NT, 4, 32))
        ro1v = ro1.rearrange("p t (i u) -> p t i u", i=4)
        ro2v = ro2.rearrange("p t (i u) -> p t i u", i=4)
        sc1 = scrC.rearrange("p (t c) -> p t c", t=NT)[
            :, :, 0:128].rearrange("p t (i u) -> p t i u", i=4)

        def rot_ops(E, O, C_, S_, first_wait):
            a1 = sp("vector", lambda e: e.tensor_tensor(
                out=ro1v, in0=E, in1=br(C_), op=ALU.mult), waits=first_wait)
            a2 = sp("vector", lambda e: e.tensor_tensor(
                out=sc1, in0=O, in1=br(S_), op=ALU.mult), waits=[a1])
            a3 = sp("vector", lambda e: e.tensor_tensor(
                out=ro1v, in0=ro1v, in1=sc1, op=ALU.subtract), waits=[a2])
            a4 = sp("vector", lambda e: e.tensor_tensor(
                out=ro2v, in0=E, in1=br(S_), op=ALU.mult), waits=[a3])
            a5 = sp("vector", lambda e: e.tensor_tensor(
                out=sc1, in0=O, in1=br(C_), op=ALU.mult), waits=[a4])
            a6 = sp("vector", lambda e: e.tensor_tensor(
                out=ro2v, in0=ro2v, in1=sc1, op=ALU.add), waits=[a5])
            return a6

        def transpose_stacks(dst1, dst2, rot_tok, mm_prev, ev_prev):
            tp_last, ev_last = mm_prev, ev_prev
            for (src, dst) in ((ro1, dst1), (ro2, dst2)):
                for tt in range(NT):
                    w = []
                    if tt % 4 == 0:
                        w = [rot_tok] + [ev_last if ev_last else tp_last]
                    tp_last = sp("tensor", lambda e, src=src, tt=tt:
                                 e.transpose(
                                     psumAb[:, 128 * (tt % 4):
                                            128 * (tt % 4) + 128],
                                     src[:, tt, :], idb), waits=w)
                    if tt % 4 == 3:
                        ev_last = sp("vector", lambda e, dst=dst, tt=tt:
                                     e.tensor_copy(
                                         out=dst[:, 128 * (tt - 3):
                                                 128 * (tt + 1)],
                                         in_=psumAb[:, 0:512]),
                                     waits=[tp_last])
            return tp_last, ev_last

        def u_mms(Ua, Ub, dst, ev_ready, first_extra):
            mm_u, uev_last = None, None
            for blk in range(4):
                for i in range(4):
                    w = []
                    if i == 0:
                        w = [uev_last] if uev_last else (
                            [ev_ready] + first_extra)
                    mm_u = sp("tensor", lambda e, i=i, blk=blk, Ua=Ua:
                              e.matmul(
                                  psumB[32 * i:32 * i + 32, 512:1024],
                                  Ua[32 * i:32 * i + 32, :],
                                  ts1[32 * i:32 * i + 32,
                                      512 * blk:512 * blk + 512],
                                  start=True, stop=False,
                                  skip_group_check=True,
                                  tile_position=(32 * i, 32 * i)), waits=w)
                for i in range(4):
                    mm_u = sp("tensor", lambda e, i=i, blk=blk, Ub=Ub:
                              e.matmul(
                                  psumB[32 * i:32 * i + 32, 512:1024],
                                  Ub[32 * i:32 * i + 32, :],
                                  ts2[32 * i:32 * i + 32,
                                      512 * blk:512 * blk + 512],
                                  start=False, stop=True,
                                  skip_group_check=True,
                                  tile_position=(32 * i, 32 * i)), waits=[])
                uev_last = sp("vector", lambda e, dst=dst, blk=blk:
                              e.tensor_copy(
                                  out=dst[:, 512 * blk:512 * blk + 512],
                                  in_=psumB[:, 512:1024]), waits=[mm_u])
            return mm_u, uev_last

        rtok_q0 = rot_ops(qEv, qOv, ctq, stq, [prev_rot_v])
        rtok_q = rtok_q0
        if debug and g == 0:
            sp("gpsimd", lambda e: e.dma_start(
                out=DBG["ro1"], in_=ro1.rearrange("p a b -> p (a b)")),
                waits=[rtok_q0], dma_out=True)
        tpq, evq = transpose_stacks(ts1, ts2,
                                    (P.sems["dOUT"], P.sems["dOUT"].count)
                                    if (debug and g == 0) else rtok_q0,
                                    prev_rot_mm, None)
        mmq, uevq = u_mms(u1qb, u2qb, qlS[g], evq, [tok_ucast])
        if debug and g == 0:
            uevq = sp("gpsimd", lambda e: e.dma_start(out=DBG["ts1"], in_=ts1),
                      waits=[uevq, evq], dma_out=True)
            uevq = (P.sems["dOUT"], P.sems["dOUT"].count)
        cp_last = uevq
        for a in range(4):
            for i in range(4):
                h = 4 * g + i
                srcE = projbf[:, h, 256 * a:256 * a + 128].rearrange(
                    "p (b u) -> p b u", b=4)
                srcO = projbf[:, h, 256 * a + 128:256 * a + 256].rearrange(
                    "p (b u) -> p b u", b=4)
                cp_last = sp("vector", lambda e, a=a, i=i, srcE=srcE:
                             e.tensor_copy(
                                 out=ket[:, 4 * a:4 * a + 4,
                                         32 * i:32 * i + 32], in_=srcE),
                             waits=[uevq] if (a == 0 and i == 0) else [])
                cp_last = sp("vector", lambda e, a=a, i=i, srcO=srcO:
                             e.tensor_copy(
                                 out=kot[:, 4 * a:4 * a + 4,
                                         32 * i:32 * i + 32], in_=srcO),
                             waits=[])
        kEv = ket.rearrange("p t (i u) -> p t i u", i=4)
        kOv = kot.rearrange("p t (i u) -> p t i u", i=4)
        rtok_k = rot_ops(kEv, kOv, ctk, stk, [cp_last])
        tpk, evk = transpose_stacks(ts1, ts2, rtok_k, mmq, None)
        mmk, uevk = u_mms(u1b, u2b, klS[g], evk, [])
        tok_qlkl[g] = uevk
        b1 = sp("vector", lambda e, g=g: e.tensor_mul(
            scrC, qlS[g], qlS[g]), waits=[uevk])
        mmn = None
        for blk in range(4):
            mmn = sp("tensor", lambda e, blk=blk: e.matmul(
                psumA[0:1, 512 * blk:512 * blk + 512], onescol,
                scrC[:, 512 * blk:512 * blk + 512], start=True, stop=True),
                waits=[b1, mmk] if blk == 0 else [])
        b2 = sp("vector", lambda e: e.reduce_max(
            out=C1(2), in_=psumA[0:1, 0:N], axis=AX.X), waits=[mmn])
        b3 = sp("vector", lambda e, g=g: e.tensor_mul(
            scrC, klS[g], klS[g]), waits=[b2])
        for blk in range(4):
            mmn = sp("tensor", lambda e, blk=blk: e.matmul(
                psumA[0:1, 512 * blk:512 * blk + 512], onescol,
                scrC[:, 512 * blk:512 * blk + 512], start=True, stop=True),
                waits=[b3] if blk == 0 else [])
        b4 = sp("vector", lambda e: e.reduce_max(
            out=C1(3), in_=psumA[0:1, 0:N], axis=AX.X), waits=[mmn])
        b5 = sp("vector", lambda e: e.tensor_mul(C1(4), C1(2), C1(3)),
                waits=[b4])
        b6 = sp("scalar", lambda e: e.activation(out=C1(5), in_=C1(4),
                                                 func=AF.Ln), waits=[b5])
        b7 = sp("scalar", lambda e: e.activation(out=C1(6), in_=C1(5),
                                                 func=AF.Exp, scale=0.5),
                waits=[b6])
        b8 = sp("vector", lambda e: e.tensor_scalar(
            out=C1(7), in0=C1(6), scalar1=-1.0, scalar2=None, op0=ALU.mult),
            waits=[b7])
        tok_gcol[g] = replicate_col(COL[f"gneg{g}"], C1(7), [b8])
        prev_rot_v = tok_gcol[g]
        prev_rot_mm = mmn

    if debug:
        dbg_guard.append(sp("gpsimd", lambda e: e.dma_start(
            out=DBG["ql0"], in_=qlS[0]), waits=[tok_qlkl[0]], dma_out=True))
        dbg_guard.append(sp("gpsimd", lambda e: e.dma_start(
            out=DBG["kl0"], in_=klS[0]), waits=[tok_qlkl[0]], dma_out=True))
        dbg_guard.append(sp("gpsimd", lambda e: e.dma_start(
            out=DBG["cols"], in_=cols), waits=[tok_gcol[3]], dma_out=True))

    tok_vset = sp("vector", lambda e: e.memset(vscr[:], 1.0),
                  waits=[tok_proj_all])
    vcp = tok_vset
    for h in range(H):
        for a in range(4):
            srcE = projbf[:, h, 256 * a:256 * a + 128].rearrange(
                "p (b u) -> p b u", b=4)
            srcO = projbf[:, h, 256 * a + 128:256 * a + 256].rearrange(
                "p (b u) -> p b u", b=4)
            vcp = sp("vector", lambda e, h=h, a=a, srcE=srcE: e.tensor_copy(
                out=vscr[:, 4 * a:4 * a + 4, h, 0:32], in_=srcE),
                waits=[tok_vset] if (h == 0 and a == 0) else [])
            vcp = sp("vector", lambda e, h=h, a=a, srcO=srcO: e.tensor_copy(
                out=vscr[:, 4 * a:4 * a + 4, h, 32:64], in_=srcO), waits=[])
    tok_vscr = vcp
    if stop_phase == "P3":
        return _finish_program(nc, stack, P)

    w2sum = cols[:, 37:38]
    w2sumt = cols[:, 38:39]
    p1b_state = {"v_last": None, "v_hist": []}
    p1b_queue = []

    def mk_p1b(w_hbm, kc, pc, wsum_col, wsumt_col, first):
        def emit():
            st = p1b_state
            if first:
                st["v_last"] = sp("vector",
                                  lambda e, c=wsum_col: e.memset(c, 0.0),
                                  waits=[tok_vscr])
            j = len(st["v_hist"])
            wdep = [st["v_last"]] if j < 2 else [st["v_hist"][j - 2]]
            dtok = sp("sync", lambda e, kc=kc, pc=pc, j=j: e.dma_start(
                out=wstrP[:, j % 2],
                in_=w_hbm[128 * kc:128 * kc + 128,
                          1024 * pc:1024 * pc + 1024]),
                waits=wdep, sem="dWa" if j % 2 == 0 else "dWb")
            vr = sp("vector", lambda e, j=j, c=wsumt_col: e.reduce_sum(
                out=c, in_=wstrP[:, j % 2], axis=AX.X,
                apply_absolute_value=True), waits=[dtok, st["v_last"]])
            va = sp("vector", lambda e, a=wsum_col, b=wsumt_col:
                    e.tensor_add(a, a, b), waits=[vr])
            st["v_last"] = va
            st["v_hist"].append(va)
        return emit

    for kc in range(8):
        for pc in range(4):
            p1b_queue.append(mk_p1b(w1T_in, kc, pc, COL["wsum"],
                                    COL["wsumt"], kc == 0 and pc == 0))
    for kc in range(32):
        p1b_queue.append(mk_p1b(w2T_in, kc, 0, w2sum, w2sumt, kc == 0))

    tok_x2 = [None] * NT
    prev_y_v = tok_vscr
    prev_ny = None
    exp_hist = []
    pvk_hist = []
    sc_hist = []
    tn_hist = []
    ny_prev_pair = None
    prev_wo_mm = prev_rot_mm
    ATT = [(qb, p, kc) for qb in range(4) for p in range(8)
           for kc in range(NT)]

    def att_sc(n):
        qb, p, kc = ATT[n]
        g = p // 2
        i0 = 2 * (p % 2)
        par = kc % 2
        w_sc = []
        if n >= 2:
            w_sc.append(exp_hist[n - 2])
        else:
            w_sc.append(tok_vscr)
        if kc == 0:
            w_sc += [tok_qlkl[g], tok_gcol[g]]
        mm_s = None
        for jj, i in enumerate((i0, i0 + 1)):
            mm_s = sp("tensor",
                      lambda e, g=g, i=i, jj=jj, kc=kc, qb=qb, par=par:
                      e.matmul(
                          psumA[:, 1024 * par + 512 * jj:
                                1024 * par + 512 * jj + 512],
                          klS[g][32 * i:32 * i + 32,
                                 128 * kc:128 * kc + 128],
                          qlS[g][32 * i:32 * i + 32,
                                 512 * qb:512 * qb + 512],
                          start=True, stop=True,
                          tile_position=(32 * i, 0)),
                      waits=w_sc if jj == 0 else [])
        sc_hist.append(mm_s)

    att_state = {"emitted": 0}

    def att_step(n):
        qb, p, kc = ATT[n]
        g = p // 2
        h0 = 4 * g + 2 * (p % 2)
        par = kc % 2
        while att_state["emitted"] <= min(n + 1, len(ATT) - 1):
            att_sc(att_state["emitted"])
            att_state["emitted"] += 1
        w_exp = [sc_hist[n]]
        if n >= 2:
            w_exp.append(pvk_hist[n - 2])
        tok_exp = sp("scalar", lambda e, g=g, par=par: e.activation(
            out=pbuf[:, par, 0:1024],
            in_=psumA[:, 1024 * par:1024 * par + 1024],
            func=AF.Exp, bias=COL[f"gneg{g}"], scale=1.0),
            waits=w_exp)
        exp_hist.append(tok_exp)
        mm_pv = None
        for jj, h in enumerate((h0, h0 + 1)):
            w_pv = [tok_exp]
            if kc == 0 and jj == 0:
                w_pv.append(prev_y_v)
                if n == 0:
                    w_pv.append(tok_vscr)
            mm_pv = sp("tensor",
                       lambda e, h=h, jj=jj, kc=kc, par=par:
                       e.matmul(
                           psumB[0:65, 512 * jj:512 * jj + 512],
                           vscr[:, kc, h, :],
                           pbuf[:, par, 512 * jj:512 * jj + 512],
                           start=(kc == 0), stop=(kc == NT - 1)),
                       waits=w_pv if jj == 0 else [])
        pvk_hist.append(mm_pv)
        return mm_pv

    for qb in range(4):
        tok_ydone = []
        pend_fin = None
        for p in range(8):
            g, j = p // 2, p % 2
            i0, i1 = 2 * j, 2 * j + 1
            h0, h1 = 4 * g + i0, 4 * g + i1
            pair_pv = None
            if pend_fin is not None:
                pend_fin()
                pend_fin = None
            for _ in range(2):
                if p1b_queue:
                    p1b_queue.pop(0)()
            for kc in range(NT):
                pair_pv = att_step((qb * 8 + p) * NT + kc)
            ev_y = []
            for jj, i in enumerate((i0, i1)):
                w_ev = [pair_pv]
                if jj == 0 and len(tn_hist) >= 2:
                    w_ev.append(tn_hist[-2])
                ev_y.append(sp("vector", lambda e, jj=jj, i=i: e.tensor_copy(
                    out=ystg[:, i, :],
                    in_=psumB[0:65, 512 * jj:512 * jj + 512]),
                    waits=w_ev))
            prev_y_v = ev_y[-1]

            def make_fin(p=p, ev_y=ev_y, i0=i0, i1=i1, h0=h0, h1=h1):
                def fin():
                    nonlocal ny_prev_pair, prev_ny
                    tn = []
                    for jj, i in enumerate((i0, i1)):
                        for sub in range(4):
                            w = []
                            if jj == 0 and sub == 0:
                                if ny_prev_pair is not None:
                                    w.append(ny_prev_pair)
                                if p == 0 and prev_ny is not None:
                                    w.append(prev_ny)
                            if sub == 0:
                                w = w + [ev_y[jj]]
                            tn.append(sp("tensor",
                                         lambda e, jj=jj, i=i, sub=sub:
                                         e.transpose(
                                psumB[:, 1024 + 512 * jj + 65 * sub:
                                      1024 + 512 * jj + 65 * sub + 65],
                                ystg[0:65, i, 128 * sub:128 * sub + 128],
                                idf[0:65, 0:65]), waits=w))
                    tn_hist.append(tn[-1])
                    ny = None
                    for jj, h in enumerate((h0, h1)):
                        pview = psumB[:, 1024 + 512 * jj:
                                      1024 + 512 * jj + 260].rearrange(
                            "p (s c) -> p s c", c=65)
                        w_rc = [tn[4 * jj + 3]] + ([ny] if ny else [])
                        rc = sp("vector", lambda e, pview=pview:
                                e.reciprocal(
                            out=rdcols, in_=pview[:, :, 64]), waits=w_rc)
                        ny = sp("vector", lambda e, h=h, pview=pview:
                                e.tensor_tensor(
                            out=ytok[:, :, h, :], in0=pview[:, :, 0:64],
                            in1=rdcols[:, :, None].to_broadcast(
                                (128, 4, 64)),
                            op=ALU.mult), waits=[rc])
                    ny_prev_pair = ny
                    prev_ny = ny
                    tok_ydone.append(ny)
                return fin

            pend_fin = make_fin()
        pend_fin()
        pend_fin = None
        for sub in range(4):
            tglob = 4 * qb + sub
            w_q = [tok_ydone[-1]]
            q1 = sp("vector", lambda e, sub=sub: e.reduce_max(
                out=COL["ma"], in_=ytok[:, sub].rearrange("p h c -> p (h c)"),
                axis=AX.X, apply_absolute_value=True), waits=w_q)
            q2 = sp("vector", lambda e: e.tensor_scalar(
                out=COL["ma"], in0=COL["ma"], scalar1=1e-6, scalar2=None,
                op0=ALU.max), waits=[q1])
            q3 = sp("vector", lambda e: e.reciprocal(out=COL["rma"],
                                                     in_=COL["ma"]),
                    waits=[q2])
            q4 = sp("vector", lambda e, sub=sub: e.tensor_scalar(
                out=may4[sub], in0=COL["ma"], scalar1=float(1.0 / 7.0),
                scalar2=None, op0=ALU.mult), waits=[q2])
            q5 = sp("vector", lambda e, sub=sub: e.tensor_scalar(
                out=scrA, in0=ytok[:, sub].rearrange("p h c -> p (h c)"),
                scalar1=COL["rma"], scalar2=7.0, op0=ALU.mult, op1=ALU.mult),
                waits=[q3])
            q6 = sp("vector", lambda e: e.tensor_scalar(
                out=rtok, in0=scrA, scalar1=MAGIC, scalar2=MAGIC,
                op0=ALU.add, op1=ALU.subtract), waits=[q5])
            tp = []
            for fc in range(8):
                w = [q6]
                if fc == 0:
                    w += [prev_ny, prev_wo_mm]
                tp.append(sp("tensor", lambda e, fc=fc: e.transpose(
                    psumBb[:, 2048 + 128 * fc:2048 + 128 * fc + 128],
                    rtok[:, 128 * fc:128 * fc + 128], idb), waits=w))
            ev = []
            for half in range(2):
                ev.append(sp("vector", lambda e, half=half, sub=sub:
                             e.tensor_copy(
                                 out=q4T.rearrange("p b k c -> p b (k c)")[
                                     :, sub % 2, 512 * half:512 * half + 512],
                                 in_=psumBb[:, 2048 + 512 * half:
                                            2048 + 512 * half + 512]),
                             waits=[tp[-1]]))
            prev_ny = ev[-1]
            mm = None
            for kc in range(8):
                for half in range(2):
                    mm = sp("tensor", lambda e, kc=kc, half=half, sub=sub:
                            e.matmul(
                                psumB[:, 512 * half:512 * half + 512],
                                q4T[:, sub % 2, kc, :],
                                sgo[:, kc, 512 * half:512 * half + 512],
                                start=(kc == 0), stop=(kc == 7)),
                            waits=[ev[0], ev[1], tok_sgo, prev_y_v]
                            if (kc == 0 and half == 0) else [])
            ep = []
            for half in range(2):
                e1 = sp("vector", lambda e, half=half, sub=sub:
                        e.scalar_tensor_tensor(
                            out=scrB[:, 512 * half:512 * half + 512],
                            in0=psumB[:, 512 * half:512 * half + 512],
                            scalar=may4[sub],
                            in1=garep[:, 512 * half:512 * half + 512],
                            op0=ALU.mult, op1=ALU.mult),
                        waits=[mm, q4, tok_garep])
                ep.append(sp("vector", lambda e, half=half, tglob=tglob:
                             e.tensor_add(
                                 xt[:, tglob, 512 * half:512 * half + 512],
                                 xt[:, tglob, 512 * half:512 * half + 512],
                                 scrB[:, 512 * half:512 * half + 512]),
                             waits=[e1]))
            tok_x2[tglob] = ep[-1]
            prev_y_v = ep[-1]
            prev_wo_mm = mm

    if stop_phase == "P4":
        return _finish_program(nc, stack, P)
    tok_dump_x2 = None
    if debug:
        tok_dump_x2 = sp("gpsimd", lambda e: e.dma_start(
            out=DBG["x2"], in_=xt.rearrange("p a b -> p (a b)")),
            waits=[tok_x2[NT - 1]], dma_out=True)

    tok_n1 = sp("sync", lambda e: e.dma_start(out=n1wrow, in_=n1w_in),
                waits=[tok_x2[NT - 1]], sem="dR")
    tok_n2 = sp("sync", lambda e: e.dma_start(out=n2wrow, in_=n2w_in),
                waits=[tok_x2[NT - 1]], sem="dR")
    tok_mr = sp("gpsimd", lambda e: e.dma_start(out=modrow2, in_=modstash),
                waits=[tok_stash,
                       (P.sems["sV"], tok_x2[NT - 1][1])])
    tok_dR = (P.sems["dR"], P.sems["dR"].count)
    tok_n1 = tok_n2 = tok_dR

    def gate2(i):
        return modrow2[0:1, D * i:D * (i + 1)]

    tok_cb1 = replicate(cb1rep, gate2(4),
                        [tok_mr, tok_x2[-1]] +
                        ([(P.sems["dOUT"], P.sems["dOUT"].count)]
                         if debug else []))
    tok_sb = replicate(sbrep, gate2(3), [tok_mr])
    tok_gb = replicate(gbrep, gate2(5), [tok_mr])
    tok_w1n = replicate(w1nrep, n1wrow, [tok_n1])
    tok_w2n = replicate(w2nrep, n2wrow, [tok_n2])

    def prep_binary(w_hbm, nkc, width, q_dst, mean_col, wsum_col,
                    tok_prev, mm_prev):
        pw = min(2048, width)
        npc = width // pw
        pieces = [(kc, pc) for kc in range(nkc) for pc in range(npc)]
        v_last = sp("vector", lambda e: e.nop(), waits=[tok_prev, p1b_state["v_last"]])
        tmm = sp("tensor", lambda e, c=wsum_col: e.transpose(
            psumA[0:1, 0:128], c, idf),
            waits=[v_last] + ([mm_prev] if mm_prev else []))
        e2 = sp("vector", lambda e: e.reduce_sum(
            out=C1(0), in_=psumA[0:1, 0:128], axis=AX.X), waits=[tmm])
        e3 = sp("vector", lambda e, nkc=nkc, width=width: e.tensor_scalar(
            out=C1(1), in0=C1(0), scalar1=float(1.0 / (nkc * 128 * width)),
            scalar2=None, op0=ALU.mult), waits=[e2])
        e4 = sp("vector", lambda e: e.tensor_scalar(
            out=C1(1), in0=C1(1), scalar1=EPS_Q, scalar2=None, op0=ALU.max),
            waits=[e3])
        e5 = sp("vector", lambda e: e.reciprocal(out=C1(2), in_=C1(1)),
                waits=[e4])
        tok_scale = replicate_col(COL["thr"], C1(2), [e5])
        tok_mean = replicate_col(mean_col, C1(1), [e5])
        q_last, q_hist = tok_scale, []
        for j, (kc, pc) in enumerate(pieces):
            wdep = [tok_mean] if j < 2 else [q_hist[j - 2]]
            dtok = sp("sync", lambda e, kc=kc, pc=pc, j=j, pw=pw: e.dma_start(
                out=wstrW[:, j % 2, 0:pw],
                in_=w_hbm[128 * kc:128 * kc + 128, pw * pc:pw * pc + pw]),
                waits=wdep, sem="dWa" if j % 2 == 0 else "dWb")
            m1 = sp("scalar", lambda e, j=j, pw=pw: e.activation(
                out=wq_scr[:, 0:pw], in_=wstrW[:, j % 2, 0:pw], func=AF.Copy,
                bias=MAGIC, scale=COL["thr"]), waits=[dtok, tok_scale, q_last])
            m2 = sp("vector", lambda e, pw=pw: e.tensor_scalar(
                out=wq_scr[:, 0:pw], in0=wq_scr[:, 0:pw], scalar1=MAGIC,
                scalar2=1.0, op0=ALU.subtract, op1=ALU.min), waits=[m1])
            q_last = sp("vector", lambda e, kc=kc, pc=pc, pw=pw: e.tensor_scalar(
                out=q_dst[:, kc, pw * pc:pw * pc + pw], in0=wq_scr[:, 0:pw],
                scalar1=-1.0, scalar2=None, op0=ALU.max), waits=[m2])
            q_hist.append(q_last)
        return q_last

    tok_w1q = prep_binary(w1T_in, 8, HID, w1q, COL["s1g"], COL["wsum"],
                          tok_w2n, prev_wo_mm)
    tok_w2q = prep_binary(w2T_in, 32, D, w2q, COL["s2g"], w2sum,
                          tok_w1q, None)
    if stop_phase == "PREP2":
        return _finish_program(nc, stack, P)

    fA = fbufA.rearrange("p a b -> p (a b)")
    st5 = {"prev_ffn_mm": prev_wo_mm}
    ffn1_last = {}
    relu_hist = {}
    h2_hist = {}
    ev2_hist = {}
    epi_hist = {}
    ev_hist = {}
    A_ctx = {}
    B_ctx = {}
    out_toks = []

    def stageA(t):
        w0 = [tok_x2[t]]
        if t >= 1 and (t - 1) in A_ctx:
            w0.append(A_ctx[t - 1]["u1t"])
        xh, _sq1 = ln_stats(xt[:, t, :], w0)
        h2a = sp("vector", lambda e: e.tensor_mul(scrA, scrA, cb1rep),
                 waits=[xh, tok_cb1])
        h2b = sp("vector", lambda e: e.tensor_add(scrA, scrA, sbrep),
                 waits=[h2a, tok_sb])
        s2 = sp("scalar", lambda e, t=t: e.activation(
            out=rbuf[:, t % 2, 0:D], in_=scrA, func=AF.Square,
            accum_out=COL["v1"]), waits=[h2b])
        d1 = sp("vector", lambda e: e.tensor_scalar(
            out=COL["v1"], in0=COL["v1"], scalar1=1.0 / D, scalar2=EPS_RMS,
            op0=ALU.mult, op1=ALU.add), waits=[s2])
        d2 = sp("scalar", lambda e: e.activation(out=COL["lnt2"],
                                                 in_=COL["v1"], func=AF.Ln),
                waits=[d1])
        d3 = sp("scalar", lambda e: e.activation(out=COL["rs1"],
                                                 in_=COL["lnt2"],
                                                 func=AF.Exp, scale=-0.5),
                waits=[d2])
        u1t = sp("vector", lambda e: e.tensor_mul(scrB, scrA, w1nrep),
                 waits=[h2b, tok_w1n])
        m1 = sp("vector", lambda e: e.reduce_max(
            out=COL["mx"], in_=scrB, axis=AX.X, apply_absolute_value=True),
            waits=[u1t])
        m2 = sp("vector", lambda e: e.tensor_tensor(
            out=COL["mx"], in0=COL["mx"], in1=COL["rs1"], op=ALU.mult),
            waits=[m1, d3])
        m3 = sp("vector", lambda e: e.tensor_scalar(
            out=COL["mx"], in0=COL["mx"], scalar1=EPS_Q, scalar2=None,
            op0=ALU.max), waits=[m2])
        m4 = sp("vector", lambda e: e.reciprocal(out=COL["mxr"],
                                                 in_=COL["mx"]), waits=[m3])
        m5 = sp("vector", lambda e: e.tensor_scalar(
            out=COL["s127"], in0=COL["mxr"], scalar1=127.0, scalar2=None,
            op0=ALU.mult), waits=[m4])
        m6 = sp("vector", lambda e: e.tensor_tensor(
            out=COL["sc"], in0=COL["s127"], in1=COL["rs1"], op=ALU.mult),
            waits=[m5])
        cnp = cols[:, 64 + t % 3:65 + t % 3]
        m7 = sp("vector", lambda e, cnp=cnp: e.tensor_scalar(
            out=cnp, in0=COL["mx"], scalar1=float(1.0 / 127.0),
            scalar2=None, op0=ALU.mult), waits=[m3])
        m8 = sp("vector", lambda e, cnp=cnp: e.tensor_tensor(
            out=cnp, in0=cnp, in1=COL["s1g"], op=ALU.mult),
            waits=[m7, tok_w1q])
        g1 = sp("vector", lambda e: e.tensor_scalar(
            out=scrB, in0=scrB, scalar1=COL["sc"], scalar2=MAGIC,
            op0=ALU.mult, op1=ALU.add), waits=[m6])
        g2 = sp("vector", lambda e: e.tensor_scalar(
            out=rtok, in0=scrB, scalar1=MAGIC, scalar2=None,
            op0=ALU.subtract), waits=[g1])
        tp = []
        for fc in range(8):
            w = [g2]
            if fc == 0:
                w.append(st5["prev_ffn_mm"])
                if t >= 1:
                    w.append(ev_hist[t - 1])
                else:
                    w.append(exp_hist[-1])
            tp.append(sp("tensor", lambda e, fc=fc: e.transpose(
                psumAb[:, 128 * fc:128 * fc + 128],
                rtok[:, 128 * fc:128 * fc + 128], idb), waits=w))
        ev = []
        for half in range(2):
            ev.append(sp("vector", lambda e, half=half, t=t: e.tensor_copy(
                out=q4T.rearrange("p b k c -> p b (k c)")[
                    :, t % 3, 512 * half:512 * half + 512],
                in_=psumAb[:, 512 * half:512 * half + 512]), waits=[tp[-1]]))
        ev_hist[t] = ev[-1]
        A_ctx[t] = {"ev": ev, "m8": m8, "u1t": u1t}

    def stageA2(t):
        ev = A_ctx[t]["ev"]
        relu_toks = []
        mm = None
        for ob in range(8):
            for kc in range(8):
                w = []
                if kc == 0:
                    w = [ev[0], ev[1], tok_w1q]
                    if ob >= 2:
                        w.append(relu_toks[ob - 2])
                mm = sp("tensor", lambda e, kc=kc, ob=ob, t=t: e.matmul(
                    psumA[:, 1024 + 512 * (ob % 2):1536 + 512 * (ob % 2)],
                    q4T[:, t % 3, kc, :],
                    w1q[:, kc, 512 * ob:512 * ob + 512],
                    start=(kc == 0), stop=(kc == 7)), waits=w)
            relu_toks.append(sp("scalar", lambda e, ob=ob, t=t: e.activation(
                out=rbuf[:, t % 2, 512 * ob:512 * ob + 512],
                in_=psumA[:, 1024 + 512 * (ob % 2):1536 + 512 * (ob % 2)],
                func=AF.Relu, scale=cols[:, 64 + t % 3:65 + t % 3]),
                waits=[mm, A_ctx[t]["m8"]]))
        ffn1_last[t] = mm
        relu_hist[t] = relu_toks[-1]

    def stageB(t):
        w_sq = [relu_hist[t]]
        if t >= 1:
            w_sq.append(h2_hist[t - 1])
        sq = sp("scalar", lambda e, t=t: e.activation(
            out=fA, in_=rbuf[:, t % 2], func=AF.Square), waits=w_sq)
        ha = sp("vector", lambda e: e.tensor_scalar(
            out=fA, in0=fA, scalar1=COL["asc"], scalar2=COL["abi"],
            op0=ALU.mult, op1=ALU.add), waits=[sq, tok_asc, tok_abi])
        s3 = sp("scalar", lambda e: e.activation(
            out=r2tok, in_=fA, func=AF.Square, accum_out=COL["v2"]),
            waits=[ha])
        u2t = sp("vector", lambda e, t=t: e.tensor_mul(
            rbuf[:, t % 2], fA, w2nrep), waits=[ha, tok_w2n])
        n1 = sp("vector", lambda e, t=t: e.reduce_max(
            out=COL["mx2"], in_=rbuf[:, t % 2], axis=AX.X,
            apply_absolute_value=True), waits=[u2t])
        c0 = sp("vector", lambda e: e.tensor_scalar(
            out=COL["v2"], in0=COL["v2"], scalar1=1.0 / HID, scalar2=EPS_RMS,
            op0=ALU.mult, op1=ALU.add), waits=[s3])
        c1 = sp("scalar", lambda e: e.activation(out=COL["lnt3"],
                                                 in_=COL["v2"], func=AF.Ln),
                waits=[c0])
        c2 = sp("scalar", lambda e: e.activation(out=COL["sqv"],
                                                 in_=COL["lnt3"],
                                                 func=AF.Exp, scale=0.5),
                waits=[c1])
        c3 = sp("scalar", lambda e: e.activation(out=COL["rs2"],
                                                 in_=COL["lnt3"],
                                                 func=AF.Exp, scale=-0.5),
                waits=[c2])
        c4 = sp("vector", lambda e: e.tensor_scalar(
            out=COL["sqv"], in0=COL["sqv"], scalar1=EPS_Q, scalar2=None,
            op0=ALU.mult), waits=[c2])
        n3 = sp("vector", lambda e: e.tensor_max(
            COL["mx2"], COL["mx2"], COL["sqv"]), waits=[n1, c4])
        n4 = sp("vector", lambda e: e.reciprocal(out=COL["mx2r"],
                                                 in_=COL["mx2"]), waits=[n3])
        n5 = sp("vector", lambda e: e.tensor_scalar(
            out=COL["sc2"], in0=COL["mx2r"], scalar1=127.0, scalar2=None,
            op0=ALU.mult), waits=[n4])
        n7 = sp("vector", lambda e: e.tensor_scalar(
            out=COL["cn2"], in0=COL["mx2"], scalar1=float(1.0 / 127.0),
            scalar2=None, op0=ALU.mult), waits=[n3])
        n7b = sp("vector", lambda e: e.tensor_tensor(
            out=COL["cn2"], in0=COL["cn2"], in1=COL["rs2"], op=ALU.mult),
            waits=[n7, c3])
        n8 = sp("vector", lambda e: e.tensor_tensor(
            out=COL["cn2"], in0=COL["cn2"], in1=COL["s2g"], op=ALU.mult),
            waits=[n7b, tok_w2q])
        h1_ = sp("vector", lambda e, t=t: e.tensor_scalar(
            out=fA, in0=rbuf[:, t % 2], scalar1=COL["sc2"], scalar2=MAGIC,
            op0=ALU.mult, op1=ALU.add), waits=[n5, s3])
        h2_w = [h1_]
        if t >= 1:
            h2_w.append(ev2_hist[t - 1])
        h2_ = sp("vector", lambda e: e.tensor_scalar(
            out=r2tok, in0=fA, scalar1=MAGIC, scalar2=None,
            op0=ALU.subtract), waits=h2_w)
        h2_hist[t] = h2_
        B_ctx[t] = {"n8": n8, "h2q": [h2_] * 4}

    def stageB2(t):
        n8 = B_ctx[t]["n8"]
        mm = None
        ev2_last = None
        h2q = B_ctx[t]["h2q"]
        for kg in range(8):
            tpg = []
            for fc in range(4):
                fcg = 4 * kg + fc
                w = []
                if fc == 0:
                    w = [h2q[kg // 2]] if kg == 0 else [mm]
                    if kg == 0:
                        w.append(st5["prev_ffn_mm"])
                tpg.append(sp("tensor", lambda e, fcg=fcg, fc=fc:
                              e.transpose(
                                  psumAb[:, 1024 + 128 * fc:
                                         1024 + 128 * fc + 128],
                                  r2tok[:, 128 * fcg:128 * fcg + 128], idb),
                              waits=w))
            ev2_last = sp("vector", lambda e, kg=kg: e.tensor_copy(
                out=r2Tm.rearrange("p b k c -> p b (k c)")[:, kg % 2, :],
                in_=psumAb[:, 1024:1536]), waits=[tpg[-1]])
            for kcl in range(4):
                for ob in range(2):
                    w = []
                    if kcl == 0 and ob == 0:
                        w = [ev2_last, tok_w2q]
                        if kg == 0:
                            w.append(epi_hist[t - 2] if t >= 2
                                     else tok_x2[NT - 1])
                    mm = sp("tensor", lambda e, kg=kg, kcl=kcl, ob=ob, t=t:
                            e.matmul(
                                psumB[:, 1024 * (t % 2) + 512 * ob:
                                      1024 * (t % 2) + 512 * ob + 512],
                                r2Tm[:, kg % 2, kcl, :],
                                w2q[:, 4 * kg + kcl, 512 * ob:512 * ob + 512],
                                start=(kg == 0 and kcl == 0),
                                stop=(kg == 7 and kcl == 3)), waits=w)
        ev2_hist[t] = ev2_last
        ep_toks = []
        for ob in range(2):
            e1 = sp("vector", lambda e, ob=ob, t=t: e.scalar_tensor_tensor(
                out=scrB[:, 512 * ob:512 * ob + 512],
                in0=psumB[:, 1024 * (t % 2) + 512 * ob:
                          1024 * (t % 2) + 512 * ob + 512],
                scalar=COL["cn2"],
                in1=gbrep[:, 512 * ob:512 * ob + 512],
                op0=ALU.mult, op1=ALU.mult), waits=[mm, n8, tok_gb])
            ep_toks.append(sp("vector", lambda e, ob=ob, t=t: e.tensor_add(
                xt[:, t, 512 * ob:512 * ob + 512],
                xt[:, t, 512 * ob:512 * ob + 512],
                scrB[:, 512 * ob:512 * ob + 512]), waits=[e1]))
        st5["prev_ffn_mm"] = mm
        epi_hist[t] = ep_toks[-1]
        out_toks.append(sp("gpsimd", lambda e, t=t: e.dma_start(
            out=out_hbm.rearrange("(t p) d -> p t d", p=128)[:, t, :],
            in_=xt[:, t, :]), waits=[ep_toks[-1]], dma_out=True))

    stageA(0)
    stageA(1)
    for t in range(NT):
        if t + 2 < NT:
            stageA(t + 2)
        if t >= 1:
            stageB(t - 1)
        stageA2(t)
        if t >= 1:
            stageB2(t - 1)
    stageB(NT - 1)
    stageB2(NT - 1)

    sp("gpsimd", lambda e: e.nop(),
       waits=[(P.sems["dOUT"], P.sems["dOUT"].count)], noinc=True)

    return _finish_program(nc, stack, P)


def _finish_program(nc, stack, P):
    P.mark_needed()

    with nc.Block() as block:
        @block.tensor
        def _(e):
            P.replay("tensor", e)

        @block.vector
        def _(e):
            P.replay("vector", e)

        @block.scalar
        def _(e):
            P.replay("scalar", e)

        @block.sync
        def _(e):
            P.replay("sync", e)

        @block.gpsimd
        def _(e):
            P.replay("gpsimd", e)

    return nc, stack


def kernel(L0_attn_x, L0_attn_Wqkv_w, L0_attn_Wqkv_b, L0_attn_U, L0_attn_Wo_w,
           L0_ffn_norm1_w, L0_ffn_w1, L0_ffn_act_scale, L0_ffn_act_bias,
           L0_ffn_norm2_w, L0_ffn_w2, L0_c, L0_wmod):
    f32 = np.float32
    perm = sigma3_perm()
    yp = y_perm()
    ct, st_, ctk_, stk_ = rotary_tables()

    U = np.asarray(L0_attn_U, f32)
    shared = dict(
        wqkvT=np.ascontiguousarray(np.asarray(L0_attn_Wqkv_w, f32).T[:, perm]),
        bq=np.ascontiguousarray(np.asarray(L0_attn_Wqkv_b, f32)[perm])[None, :],
        woT=np.ascontiguousarray(np.asarray(L0_attn_Wo_w, f32).T[yp, :]),
        w1T=np.ascontiguousarray(np.asarray(L0_ffn_w1, f32).T),
        w2T=np.ascontiguousarray(np.asarray(L0_ffn_w2, f32).T),
        u1rep=np.ascontiguousarray(np.tile(U[0:32], (4, 1))),
        u2rep=np.ascontiguousarray(np.tile(U[32:64], (4, 1))),
        wmod=np.ascontiguousarray(np.asarray(L0_wmod, f32)),
        n1w=np.asarray(L0_ffn_norm1_w, f32)[None, :],
        n2w=np.asarray(L0_ffn_norm2_w, f32)[None, :],
        asc=np.asarray(L0_ffn_act_scale, f32).reshape(1, 1),
        abi=np.asarray(L0_ffn_act_bias, f32).reshape(1, 1),
        idf=np.eye(128, dtype=f32),
        idb=np.eye(128, dtype=f32).astype(ml_dtypes.bfloat16),
        ctq=ct, stq=st_, ctk=ctk_, stk=stk_,
        onesrow=np.ones((1, 128), f32),
        onescol=np.ones((128, 1), f32).astype(ml_dtypes.bfloat16),
    )

    x = np.asarray(L0_attn_x, f32)
    c = np.asarray(L0_c, f32)
    in_maps = []
    for b in range(8):
        m = dict(shared)
        m["x"] = np.ascontiguousarray(x[b])
        m["c"] = np.ascontiguousarray(c[b])[None, :]
        in_maps.append(m)

    nc, stack = build_program()
    try:
        res = run_bass_kernel_spmd(nc, in_maps, core_ids=list(range(8)))
    finally:
        stack.close()
    return np.stack([np.asarray(res.results[b]["out"]) for b in range(8)])




# version 21 (final)
# speedup vs baseline: 1.0246x; 1.0246x over previous
import numpy as np
import ml_dtypes
from contextlib import ExitStack

import concourse.bass as bass
import concourse.mybir as mybir
from concourse.bass_utils import run_bass_kernel_spmd

F32 = mybir.dt.float32
BF16 = mybir.dt.bfloat16
FP8 = mybir.dt.float8e4
U8 = mybir.dt.uint8
AF = mybir.ActivationFunctionType
ALU = mybir.AluOpType
AX = mybir.AxisListType

MAGIC = float(np.float32(1.5 * 2.0 ** 23))
H, DK, RNK, N, D, HID = 16, 64, 32, 2048, 1024, 4096
NT = 16
EPS_LN, EPS_RMS, EPS_Q = 1e-5, 1e-8, 1e-5
KB = 1024

ENGINES = ("tensor", "vector", "scalar", "sync", "gpsimd")


def sigma3_perm():
    perm = np.zeros(D, dtype=np.int64)
    for g in range(4):
        for i in range(4):
            h = 4 * g + i
            for u in range(32):
                perm[256 * g + 32 * i + u] = 64 * h + 2 * u
                perm[256 * g + 128 + 32 * i + u] = 64 * h + 2 * u + 1
    return perm


def y_perm():
    yp = np.zeros(D, dtype=np.int64)
    for h in range(H):
        for u in range(32):
            yp[64 * h + u] = 64 * h + 2 * u
            yp[64 * h + 32 + u] = 64 * h + 2 * u + 1
    return yp


def rotary_tables():
    half = DK // 2
    inv = 1.0 / 10000.0 ** (np.arange(half, dtype=np.float64) / half)
    ang = np.arange(N, dtype=np.float64)[:, None] * inv[None, :]
    cos = np.cos(ang).astype(np.float32)
    sin = np.sin(ang).astype(np.float32)
    mt = np.zeros(N, dtype=np.int64)
    for r in range(16):
        for q in range(128):
            mt[128 * r + q] = 16 * q + r
    cosk, sink = cos[mt], sin[mt]

    def tok(a):
        return np.ascontiguousarray(
            a.reshape(NT, 128, 32).transpose(1, 0, 2)).astype(ml_dtypes.bfloat16)

    return tok(cos), tok(sin), tok(cosk), tok(sink)


def sE(r):
    return 256 * (r // 4) + 32 * (r % 4)


class Sem:
    def __init__(self, handle, name):
        self.h = handle
        self.name = name
        self.count = 0


class Step:
    __slots__ = ("fn", "waits", "inc_sem", "inc_amt", "token", "needed")

    def __init__(self, fn, waits, inc_sem, inc_amt):
        self.fn = fn
        self.waits = waits
        self.inc_sem = inc_sem
        self.inc_amt = inc_amt
        self.token = None
        self.needed = False


class Plan:
    def __init__(self, nc, stack):
        self.steps = {e: [] for e in ENGINES}
        self.sems = {}
        for name in ("sT", "sV", "sS", "dX", "dWa", "dWb", "dWc", "dWd",
                     "dR", "dMa", "dMb", "dG", "dOUT"):
            self.sems[name] = Sem(stack.enter_context(nc.semaphore(name)), name)
        self.eng_sem = {"tensor": self.sems["sT"], "vector": self.sems["sV"],
                        "scalar": self.sems["sS"], "sync": self.sems["dX"],
                        "gpsimd": self.sems["dG"]}

    def step(self, eng, fn, waits=(), dma_out=False, sem=None, noinc=False):
        if sem is not None:
            sem = self.sems[sem]
        else:
            sem = self.sems["dOUT"] if dma_out else self.eng_sem[eng]
        amt = 16 if eng in ("sync", "gpsimd") or dma_out else 1
        w = []
        for t in waits:
            if t is None:
                continue
            s, v = t
            assert v <= s.count, f"wait on future of {s.name}: {v} > {s.count}"
            w.append((s, v))
        st = Step(fn, w, sem, amt)
        if noinc:
            st.inc_amt = 0
        else:
            sem.count += amt
        st.token = (sem, sem.count)
        self.steps[eng].append(st)
        return st.token

    def mark_needed(self):
        maxw = {}
        for e in ENGINES:
            for st in self.steps[e]:
                for (s, v) in st.waits:
                    maxw[s.name] = max(maxw.get(s.name, 0), v)
        for e in ENGINES:
            for st in self.steps[e]:
                st.needed = st.inc_amt > 0

    def replay(self, eng, engobj):
        for st in self.steps[eng]:
            for (s, v) in st.waits:
                engobj.wait_ge(s.h, v)
            inst = st.fn(engobj)
            if st.needed:
                assert inst is not None, f"needed step on {eng} returned None"
                inst.then_inc(st.inc_sem.h, st.inc_amt)


def build_program(debug=False, stop_phase=None):
    nc = bass.Bass("TRN2", target_bir_lowering=False)
    stack = ExitStack()

    def din(name, shape, dtype=F32):
        return nc.dram_tensor(name, list(shape), dtype,
                              kind="ExternalInput").ap()

    x_in = din("x", [N, D])
    c_in = din("c", [1, D])
    wqkvT = din("wqkvT", [D, D])
    bq_in = din("bq", [1, D])
    woT_in = din("woT", [D, D])
    w1T_in = din("w1T", [D, HID])
    w2T_in = din("w2T", [HID, D])
    u1rep_in = din("u1rep", [128, RNK])
    u2rep_in = din("u2rep", [128, RNK])
    wmod_in = din("wmod", [D, 6 * D])
    n1w_in = din("n1w", [1, D])
    n2w_in = din("n2w", [1, HID])
    asc_in = din("asc", [1, 1])
    abi_in = din("abi", [1, 1])
    idf_in = din("idf", [128, 128], F32)
    idb_in = din("idb", [128, 128], BF16)
    ctq_in = din("ctq", [128, NT, 32], BF16)
    stq_in = din("stq", [128, NT, 32], BF16)
    ctk_in = din("ctk", [128, NT, 32], BF16)
    stk_in = din("stk", [128, NT, 32], BF16)
    onesrow_in = din("onesrow", [1, 128], F32)
    onescol_in = din("onescol", [128, 1], BF16)

    out_hbm = nc.dram_tensor("out", [N, D], F32, kind="ExternalOutput").ap()
    modstash = nc.dram_tensor("modstash", [1, 6 * D], F32, kind="Internal").ap()
    DBG = {}
    if debug:
        DBG["proj"] = nc.dram_tensor("d_proj", [128, NT * D], BF16,
                                     kind="ExternalOutput").ap()
        DBG["ql0"] = nc.dram_tensor("d_ql0", [128, N], BF16,
                                    kind="ExternalOutput").ap()
        DBG["kl0"] = nc.dram_tensor("d_kl0", [128, N], BF16,
                                    kind="ExternalOutput").ap()
        DBG["cols"] = nc.dram_tensor("d_cols", [128, 96], F32,
                                     kind="ExternalOutput").ap()
        DBG["ytok"] = nc.dram_tensor("d_ytok", [128, 4 * H * 64], F32,
                                     kind="ExternalOutput").ap()
        DBG["x2"] = nc.dram_tensor("d_x2", [128, NT * D], F32,
                                   kind="ExternalOutput").ap()
        DBG["r2t0"] = nc.dram_tensor("d_r2t0", [128, HID], BF16,
                                     kind="ExternalOutput").ap()
        DBG["ro1"] = nc.dram_tensor("d_ro1", [128, NT * 128], BF16,
                                    kind="ExternalOutput").ap()
        DBG["ts1"] = nc.dram_tensor("d_ts1", [128, N], BF16,
                                    kind="ExternalOutput").ap()
        DBG["mod"] = nc.dram_tensor("d_mod", [1, 6 * D], F32,
                                    kind="ExternalOutput").ap()

    def sb(name, shape, dtype):
        return stack.enter_context(nc.sbuf_tensor(name, list(shape), dtype)).ap()

    xt = sb("xt", [128, NT, D], F32)
    idf = sb("idf_t", [128, 128], F32)
    idb = sb("idb_t", [128, 128], BF16)
    onesrow = sb("onesrow_t", [1, 128], F32)
    onescol = sb("onescol_t", [128, 1], BF16)
    u1b = sb("u1b", [128, RNK], BF16)
    u2b = sb("u2b", [128, RNK], BF16)
    u1qb = sb("u1qb", [128, RNK], BF16)
    u2qb = sb("u2qb", [128, RNK], BF16)
    u1f = sb("u1f", [128, RNK], F32)
    u2f = sb("u2f", [128, RNK], F32)
    siluc = sb("siluc", [128, 8], F32)
    silucb = sb("silucb", [128, 8], BF16)
    cols = sb("cols", [128, 96], F32)
    cells = sb("cells", [1, 64], F32)
    scrA = sb("scrA", [128, D], F32)
    scrB = sb("scrB", [128, D], F32)
    q4T = sb("q4T", [128, 3, 8, 128], BF16)
    rtok = sb("rtok", [128, D], BF16)
    r2Tm = sb("r2Tm", [128, 2, 4, 128], BF16)
    ARENA_KB = 123
    arena = sb("arena", [128, ARENA_KB * KB], U8)

    def reg(off_kb, shape, dtype):
        n = int(np.prod(shape[1:]))
        nbytes = n * mybir.dt.size(dtype)
        assert off_kb * KB + nbytes <= ARENA_KB * KB, f"arena overflow @{off_kb}"
        v = arena[:, off_kb * KB: off_kb * KB + nbytes].bitcast(dtype)
        if len(shape) == 3:
            v = v.rearrange("p (a b) -> p a b", a=shape[1])
        elif len(shape) == 4:
            v = v.rearrange("p (a b c) -> p a b c", a=shape[1], b=shape[2])
        return v

    wmodb = reg(0, [128, 2, 2 * D], BF16)
    modrow = reg(24, [128, 6 * D], F32)[0:1]
    silu_t = reg(48, [128, 2, D], F32)[0:1]
    gsel = reg(56, [128, 3 * D], F32)[0:1]
    crow = reg(84, [128, D], F32)[0:1]
    wstr2 = reg(8, [128, 2, D], F32)
    wabs2 = reg(16, [128, D], BF16)
    arow1 = reg(18, [128, D], F32)[0:1]
    arow2 = reg(52, [128, D], F32)[0:1]
    projbf = reg(0, [128, NT, D], BF16)
    ca1rep = reg(68, [128, D], F32)
    sarep = reg(72, [128, D], F32)
    aqrep = reg(76, [128, D], F32)
    bqrep = reg(80, [128, D], F32)
    sgq = reg(88, [128, 8, D], FP8)
    qlS = [reg(33 + 4 * g, [128, N], BF16) for g in range(4)]
    klS = [reg(49 + 4 * g, [128, N], BF16) for g in range(4)]
    ket = reg(65, [128, NT, 128], BF16)
    kot = reg(69, [128, NT, 128], BF16)
    ro1 = reg(73, [128, NT, 128], BF16)
    ro2 = reg(77, [128, NT, 128], BF16)
    ts1 = reg(81, [128, N], BF16)
    ts2 = reg(85, [128, N], BF16)
    scrC = reg(89, [128, N], BF16)
    ctq = reg(110, [128, NT, 32], BF16)
    stq = reg(111, [128, NT, 32], BF16)
    ctk = reg(112, [128, NT, 32], BF16)
    stk = reg(113, [128, NT, 32], BF16)
    sgo = reg(98, [128, 8, D], FP8)
    wstrP = reg(110, [128, 2, 1024], F32)
    garep = reg(106, [128, D], F32)
    ctabs_hi = 110
    vscr = reg(65, [128, NT, H, 65], BF16)
    ytok = reg(0, [128, 4, H, 64], F32)
    ystg = reg(16, [128, 4, 512], F32)[0:65]
    pbuf = reg(24, [128, 2, N], BF16)
    n1wrow = reg(0, [128, D], F32)[0:1]
    n2wrow = reg(4, [128, HID], F32)[0:1]
    modrow2 = reg(20, [128, 6 * D], F32)[0:1]
    cb1rep = reg(48, [128, D], BF16)
    sbrep = reg(50, [128, D], BF16)
    gbrep = reg(52, [128, D], BF16)
    w1nrep = reg(54, [128, D], F32)
    w2nrep = reg(58, [128, HID], BF16)
    wstrW = reg(0, [128, 4, 1024], F32)
    w2q = reg(16, [128, 32, D], FP8)
    wq_scr = reg(66, [128, 2048], F32)
    wabsW = reg(74, [128, 2048], BF16)
    w1q = reg(82, [128, 8, HID], FP8)
    fbufA = reg(0, [128, 8, 512], F32)
    rbuf = reg(66, [128, 2, HID], BF16)
    r2tok = reg(114, [128, HID], BF16)

    psumA = stack.enter_context(nc.psum_tensor("psumA", [128, N], F32)).ap()
    psumB = stack.enter_context(nc.psum_tensor("psumB", [128, N], F32)).ap()
    psumAb = psumA.bitcast(BF16)
    psumBb = psumB.bitcast(BF16)

    P = Plan(nc, stack)
    sp = P.step

    COLNAMES = ["sx", "ssq", "mu", "mu2", "var", "lnt", "rstd", "bias",
                "ma", "rma", "v1", "lnt2", "rs1", "mx", "mxr", "s127",
                "sc", "cn", "v2", "lnt3", "rs2", "mx2", "mx2r", "s127b",
                "sc2", "cn2", "asc", "abi", "thr", "s1g", "s2g", "wsum", "wsumt",
                "gneg0", "gneg1", "gneg2", "gneg3", "sqv"]
    COL = {nme: cols[:, i:i + 1] for i, nme in enumerate(COLNAMES)}
    maq16 = [cols[:, 40 + t:41 + t] for t in range(NT)]
    may4 = [cols[:, 56 + s:57 + s] for s in range(4)]
    rdcols = cols[:, 60:64]
    C1 = lambda i: cells[0:1, i:i + 1]

    tok_x = sp("sync", lambda e: e.dma_start(
        out=xt[:], in_=x_in.rearrange("(t p) d -> p t d", p=128)))
    tok_consts = None
    for dst, src in ((idf, idf_in), (idb, idb_in), (ctq, ctq_in),
                     (stq, stq_in), (ctk, ctk_in), (stk, stk_in),
                     (onesrow, onesrow_in), (onescol, onescol_in),
                     (u1f, u1rep_in), (u2f, u2rep_in), (crow, c_in),
                     (cells[0:1, 16:17], asc_in), (cells[0:1, 17:18], abi_in)):
        tok_consts = sp("sync", lambda e, d=dst, s=src: e.dma_start(out=d, in_=s))

    tok_S0 = (P.sems["dX"], P.sems["dX"].count)
    tok_x = tok_consts = tok_S0

    tok_cols0 = sp("vector", lambda e: e.memset(cols[:], 0.0))
    tok_ucast = tok_cols0
    for dst, src, scl in ((u1b, u1f, 1.0), (u2b, u2f, 1.0),
                          (u1qb, u1f, 0.125), (u2qb, u2f, 0.125)):
        tok_ucast = sp("vector", lambda e, d=dst, s=src, sc=scl:
                       e.tensor_scalar(out=d, in0=s, scalar1=sc, scalar2=0.0,
                                       op0=ALU.mult, op1=ALU.add),
                       waits=[tok_consts])

    rep_state = {"prev_ev": None}

    def replicate(dst, src_row, waits):
        nb = dst.shape[-1] // 512
        last = None
        for b in range(nb):
            ps = psumB[:, 512 * (b % 2):512 * (b % 2) + 512]
            w = list(waits) + ([rep_state["prev_ev"]]
                               if rep_state["prev_ev"] else [])
            tmm = sp("tensor", lambda e, ps=ps, b=b, s=src_row: e.matmul(
                ps, onesrow, s[0:1, 512 * b:512 * b + 512],
                start=True, stop=True), waits=w if b == 0 else [last])
            last = sp("vector", lambda e, ps=ps, b=b, d=dst: e.tensor_copy(
                out=d[:, 512 * b:512 * b + 512], in_=ps), waits=[tmm])
            rep_state["prev_ev"] = last
        return last

    def replicate_col(dst_col, src11, waits):
        w = list(waits) + ([rep_state["prev_ev"]]
                           if rep_state["prev_ev"] else [])
        tmm = sp("tensor", lambda e, s=src11: e.matmul(
            psumB[:, 1536:1537], onesrow, s, start=True, stop=True), waits=w)
        last = sp("vector", lambda e, d=dst_col: e.tensor_copy(
            out=d, in_=psumB[:, 1536:1537]), waits=[tmm])
        rep_state["prev_ev"] = last
        return last

    tok_asc = replicate_col(COL["asc"], cells[0:1, 16:17], [tok_consts])
    tok_abi = replicate_col(COL["abi"], cells[0:1, 17:18], [tok_consts])

    t1 = sp("scalar", lambda e: e.activation(out=silu_t[0:1, 0, :], in_=crow,
                                             func=AF.Exp, scale=-1.0),
            waits=[tok_consts])
    t2 = sp("vector", lambda e: e.tensor_scalar(
        out=silu_t[0:1, 0, :], in0=silu_t[0:1, 0, :], scalar1=1.0,
        scalar2=None, op0=ALU.add), waits=[t1])
    t3 = sp("vector", lambda e: e.reciprocal(out=silu_t[0:1, 1, :],
                                             in_=silu_t[0:1, 0, :]),
            waits=[t2])
    tok_silu = sp("vector", lambda e: e.tensor_mul(
        silu_t[0:1, 0, :], crow, silu_t[0:1, 1, :]), waits=[t3])
    tp8 = []
    for k in range(8):
        tp8.append(sp("tensor", lambda e, k=k: e.transpose(
            psumA[:, k:k + 1], silu_t[0:1, 0, 128 * k:128 * k + 128],
            idf[0:1, 0:1]), waits=[tok_silu] if k == 0 else []))
    tok_sc1 = sp("vector", lambda e: e.tensor_copy(out=siluc,
                                                   in_=psumA[:, 0:8]),
                 waits=[tp8[-1]])
    tok_sc2 = sp("vector", lambda e: e.tensor_copy(out=silucb, in_=siluc),
                 waits=[tok_sc1])

    def mod_ps(j):
        if j < 2:
            return psumA[0:1, 512 * j:512 * j + 512]
        return psumB[0:1, 512 * (j - 2):512 * (j - 2) + 512]

    def prep_ternary_steps(w_hbm, sign_dst, alpha_row, tok_prev):
        st = {"mm_last": None, "v_last": tok_prev, "v_hist": [],
              "sg_last": None, "sg_hist": [], "thr": None}

        def pass1_chunk(kc):
            wdep = [st["v_last"]] if kc < 2 else [st["v_hist"][kc - 2]]
            dtok = sp("sync", lambda e, kc=kc: e.dma_start(
                out=wstr2[:, kc % 2], in_=w_hbm[128 * kc:128 * kc + 128, :]),
                waits=wdep, sem="dWa" if kc % 2 == 0 else "dWb")
            vn = sp("vector", lambda e, kc=kc: e.tensor_scalar(
                out=scrA, in0=wstr2[:, kc % 2], scalar1=-1.0, scalar2=None,
                op0=ALU.mult), waits=[dtok, st["v_last"]])
            v_last = sp("vector", lambda e, kc=kc: e.tensor_max(
                wabs2, scrA, wstr2[:, kc % 2]),
                waits=[vn] + ([st["mm_last"]] if st["mm_last"] else []))
            st["v_last"] = v_last
            st["v_hist"].append(v_last)
            for b2 in range(2):
                st["mm_last"] = sp("tensor", lambda e, kc=kc, b2=b2: e.matmul(
                    psumB[0:1, 1024 + 512 * b2:1024 + 512 * b2 + 512],
                    onescol, wabs2[:, 512 * b2:512 * b2 + 512],
                    start=(kc == 0), stop=(kc == 7)),
                    waits=[v_last] if b2 == 0 else [])

        def mid():
            e1 = sp("vector", lambda e: e.tensor_scalar(
                out=alpha_row, in0=psumB[0:1, 1024:1024 + D],
                scalar1=1.0 / D, scalar2=None, op0=ALU.mult),
                waits=[st["mm_last"]])
            e2 = sp("vector", lambda e: e.reduce_sum(
                out=C1(0), in_=psumB[0:1, 1024:1024 + D], axis=AX.X),
                waits=[st["mm_last"]])
            e3 = sp("vector", lambda e: e.tensor_scalar(
                out=C1(1), in0=C1(0), scalar1=float(0.05 / (D * D)),
                scalar2=None, op0=ALU.mult), waits=[e2])
            st["thr"] = replicate_col(COL["thr"], C1(1), [e3])
            st["sg_last"] = e1

        def pass2_chunk(kc):
            wdep = [st["thr"]] if kc < 2 else [st["sg_hist"][kc - 2]]
            dtok = sp("sync", lambda e, kc=kc: e.dma_start(
                out=wstr2[:, kc % 2], in_=w_hbm[128 * kc:128 * kc + 128, :]),
                waits=wdep, sem="dWa" if kc % 2 == 0 else "dWb")
            g1 = sp("vector", lambda e, kc=kc: e.tensor_scalar(
                out=scrA, in0=wstr2[:, kc % 2], scalar1=COL["thr"],
                scalar2=None, op0=ALU.is_ge),
                waits=[dtok, st["thr"], st["sg_last"]])
            g2 = sp("vector", lambda e, kc=kc: e.tensor_scalar(
                out=scrB, in0=wstr2[:, kc % 2], scalar1=-1.0, scalar2=None,
                op0=ALU.mult), waits=[dtok])
            g3 = sp("vector", lambda e: e.tensor_scalar(
                out=scrB, in0=scrB, scalar1=COL["thr"], scalar2=None,
                op0=ALU.is_ge), waits=[g2])
            sg_last = sp("vector", lambda e, kc=kc: e.tensor_sub(
                sign_dst[:, kc, :], scrA, scrB), waits=[g1, g3])
            st["sg_last"] = sg_last
            st["sg_hist"].append(sg_last)

        return pass1_chunk, mid, pass2_chunk, st

    p1c, pmid, p2c, prep_st = prep_ternary_steps(wqkvT, sgq, arow1, tok_abi)
    tok_mod = None
    mm_kc = []
    mm_prev_round = None
    for rnd, (c0, nob) in enumerate(((0, 4), (2048, 4), (4096, 4))):
        wm_toks, mm_kc = [], []
        ev_prev_round = tok_mod
        for kc in range(8):
            wdep = [] if (rnd == 0 and kc < 2) else (
                [mm_kc[kc - 2]] if kc >= 2 else [mm_prev_round])
            wm_toks.append(sp("gpsimd", lambda e, kc=kc, c0=c0, nob=nob:
                              e.dma_start(
                out=wmodb[:, kc % 2, 0:512 * nob],
                in_=wmod_in[128 * kc:128 * kc + 128, c0:c0 + 512 * nob]),
                waits=wdep, sem="dMa" if kc % 2 == 0 else "dMb"))
            last = None
            for j in range(nob):
                w = []
                if j == 0:
                    w = [wm_toks[kc], tok_sc2, tp8[-1]]
                    if kc == 0 and ev_prev_round is not None:
                        w.append(ev_prev_round)
                last = sp("tensor", lambda e, kc=kc, j=j: e.matmul(
                    mod_ps(j), silucb[:, kc:kc + 1],
                    wmodb[:, kc % 2, 512 * j:512 * j + 512],
                    start=(kc == 0), stop=(kc == 7)), waits=w)
            mm_kc.append(last)
            if rnd == 0:
                p1c(kc)
            elif rnd == 1:
                p2c(kc)
        mm_prev_round = mm_kc[-1]
        for j in range(nob):
            tok_mod = sp("vector", lambda e, j=j, c0=c0: e.tensor_copy(
                out=modrow[0:1, c0 + 512 * j:c0 + 512 * j + 512],
                in_=mod_ps(j)), waits=[mm_kc[-1]])
        if rnd == 0:
            pmid()
    tok_sgq = prep_st["sg_last"]

    def gate(i):
        return modrow[0:1, D * i:D * (i + 1)]

    tok_gates = tok_mod
    for gi in (2, 5):
        a = sp("vector", lambda e, gi=gi: e.tensor_scalar(
            out=gsel[0:1, 0:D], in0=gate(gi), scalar1=0.0, scalar2=None,
            op0=ALU.is_ge), waits=[tok_gates])
        a2 = sp("vector", lambda e: e.tensor_scalar(
            out=gsel[0:1, 2 * D:3 * D], in0=gsel[0:1, 0:D], scalar1=2.0,
            scalar2=-1.0, op0=ALU.mult, op1=ALU.add), waits=[a])
        m0 = sp("vector", lambda e, gi=gi: e.tensor_scalar(
            out=gsel[0:1, D:2 * D], in0=gate(gi), scalar1=-1.0,
            scalar2=None, op0=ALU.mult), waits=[tok_gates])
        m1_ = sp("vector", lambda e, gi=gi: e.tensor_max(
            gsel[0:1, D:2 * D], gsel[0:1, D:2 * D], gate(gi)), waits=[m0])
        m = sp("vector", lambda e: e.tensor_scalar(
            out=gsel[0:1, D:2 * D], in0=gsel[0:1, D:2 * D], scalar1=0.25,
            scalar2=None, op0=ALU.max), waits=[m1_])
        tok_gates = sp("vector", lambda e, gi=gi: e.tensor_mul(
            gate(gi), gsel[0:1, 2 * D:3 * D], gsel[0:1, D:2 * D]),
            waits=[a2, m])
    for gi in (1, 4):
        tok_gates = sp("vector", lambda e, gi=gi: e.tensor_scalar(
            out=gate(gi), in0=gate(gi), scalar1=1.0, scalar2=None,
            op0=ALU.add), waits=[tok_gates])
    tok_stash = sp("gpsimd", lambda e: e.dma_start(out=modstash, in_=modrow),
                   waits=[tok_gates])

    tok_ca1 = replicate(ca1rep, gate(1), [tok_gates])
    tok_sa = replicate(sarep, gate(0), [tok_gates])
    tok_aqrep = replicate(aqrep, arow1, [tok_sgq])
    tok_bq0 = sp("sync", lambda e: e.dma_start(out=crow, in_=bq_in),
                 waits=[tok_silu], sem="dR")
    tok_bqrep = replicate(bqrep, crow, [tok_bq0])
    p1c2, pmid2, p2c2, prep_st2 = prep_ternary_steps(
        woT_in, sgo, arow2, tok_aqrep)
    for kc in range(8):
        p1c2(kc)
    pmid2()
    for kc in range(8):
        p2c2(kc)
    tok_sgo = prep_st2["sg_last"]
    if stop_phase == "S2":
        return _finish_program(nc, stack, P)

    def ln_stats(xtile, w0):
        r1 = sp("vector", lambda e: e.reduce_sum(out=COL["sx"], in_=xtile,
                                                 axis=AX.X), waits=w0)
        s1 = sp("scalar", lambda e: e.activation(out=scrA, in_=xtile,
                                                 func=AF.Square,
                                                 accum_out=COL["ssq"]),
                waits=w0)
        c1 = sp("vector", lambda e: e.tensor_scalar(
            out=COL["mu"], in0=COL["sx"], scalar1=1.0 / D, scalar2=None,
            op0=ALU.mult), waits=[r1])
        c2 = sp("vector", lambda e: e.tensor_mul(COL["mu2"], COL["mu"],
                                                 COL["mu"]), waits=[c1])
        c3 = sp("vector", lambda e: e.scalar_tensor_tensor(
            out=COL["var"], in0=COL["ssq"], scalar=1.0 / D, in1=COL["mu2"],
            op0=ALU.mult, op1=ALU.subtract), waits=[s1, c2])
        c4 = sp("vector", lambda e: e.tensor_scalar(
            out=COL["var"], in0=COL["var"], scalar1=EPS_LN, scalar2=None,
            op0=ALU.add), waits=[c3])
        c5 = sp("scalar", lambda e: e.activation(out=COL["lnt"],
                                                 in_=COL["var"], func=AF.Ln),
                waits=[c4])
        c6 = sp("scalar", lambda e: e.activation(out=COL["rstd"],
                                                 in_=COL["lnt"], func=AF.Exp,
                                                 scale=-0.5), waits=[c5])
        xh = sp("vector", lambda e: e.tensor_scalar(
            out=scrA, in0=xtile, scalar1=COL["mu"], scalar2=COL["rstd"],
            op0=ALU.subtract, op1=ALU.mult), waits=[c6])
        return xh, s1

    tok_proj = [None] * NT
    q5_hist = {}
    mm1_hist = {}
    ep1_hist = {}
    prev_ev = None
    pend_ep = None
    for t in range(NT):
        w0 = [tok_x, q5_hist[t - 1] if t >= 1 else tok_sgo]
        xh, sq1 = ln_stats(xt[:, t, :], w0)
        h1a = sp("vector", lambda e: e.tensor_mul(scrA, scrA, ca1rep),
                 waits=[xh, tok_ca1])
        h1b = sp("vector", lambda e: e.tensor_add(scrA, scrA, sarep),
                 waits=[h1a, tok_sa])
        q1 = sp("vector", lambda e: e.reduce_max(
            out=COL["ma"], in_=scrA, axis=AX.X, apply_absolute_value=True),
            waits=[h1b])
        q2 = sp("vector", lambda e: e.tensor_scalar(
            out=COL["ma"], in0=COL["ma"], scalar1=1e-6, scalar2=None,
            op0=ALU.max), waits=[q1])
        q3 = sp("vector", lambda e: e.reciprocal(out=COL["rma"],
                                                 in_=COL["ma"]), waits=[q2])
        q4 = sp("vector", lambda e, t=t: e.tensor_scalar(
            out=maq16[t], in0=COL["ma"], scalar1=float(1.0 / 7.0),
            scalar2=None, op0=ALU.mult), waits=[q2])
        q5 = sp("vector", lambda e: e.tensor_scalar(
            out=scrB, in0=scrA, scalar1=COL["rma"], scalar2=7.0,
            op0=ALU.mult, op1=ALU.mult), waits=[q3, h1b])
        q5_hist[t] = q5
        q6 = sp("vector", lambda e: e.tensor_scalar(
            out=rtok, in0=scrB, scalar1=MAGIC, scalar2=MAGIC,
            op0=ALU.add, op1=ALU.subtract), waits=[q5])
        tp = []
        for fc in range(8):
            w = [q6]
            if fc == 0:
                w.append(prev_ev if prev_ev else mm_kc[-1])
                if t >= 2:
                    w.append(mm1_hist[t - 2])
            tp.append(sp("tensor", lambda e, fc=fc: e.transpose(
                psumAb[:, 128 * fc:128 * fc + 128],
                rtok[:, 128 * fc:128 * fc + 128], idb), waits=w))
        ev = []
        for half in range(2):
            ev.append(sp("vector", lambda e, half=half, t=t: e.tensor_copy(
                out=q4T.rearrange("p b k c -> p b (k c)")[
                    :, t % 2, 512 * half:512 * half + 512],
                in_=psumAb[:, 512 * half:512 * half + 512]), waits=[tp[-1]]))
        prev_ev = ev[-1]
        ps1 = psumA[:, 1024:2048] if t % 2 else psumB[:, 0:1024]
        mm = None
        for kc in range(8):
            for half in range(2):
                w = []
                if kc == 0 and half == 0:
                    w = [ev[0], ev[1], tok_sgq,
                         ep1_hist[t - 2] if t >= 2 else tok_bqrep]
                    if t == 12:
                        w.append(tok_stash)
                mm = sp("tensor", lambda e, kc=kc, half=half, t=t,
                        ps1=ps1: e.matmul(
                    ps1[:, 512 * half:512 * half + 512],
                    q4T[:, t % 2, kc, :],
                    sgq[:, kc, 512 * half:512 * half + 512],
                    start=(kc == 0), stop=(kc == 7)), waits=w)
        mm1_hist[t] = mm

        def emit_ep(t=t, mm=mm, q4=q4, ps1=ps1):
            ep = []
            for half in range(2):
                e1 = sp("vector", lambda e, half=half, t=t, ps1=ps1:
                        e.scalar_tensor_tensor(
                    out=scrB[:, 512 * half:512 * half + 512],
                    in0=ps1[:, 512 * half:512 * half + 512],
                    scalar=maq16[t],
                    in1=aqrep[:, 512 * half:512 * half + 512],
                    op0=ALU.mult, op1=ALU.mult),
                    waits=[mm, q4, tok_aqrep])
                ep.append(sp("vector", lambda e, half=half, t=t:
                             e.tensor_add(
                    projbf[:, t, 512 * half:512 * half + 512],
                    scrB[:, 512 * half:512 * half + 512],
                    bqrep[:, 512 * half:512 * half + 512]),
                    waits=[e1, tok_bqrep]))
            ep1_hist[t] = ep[-1]
            tok_proj[t] = ep[-1]

        if pend_ep is not None:
            pend_ep()
        pend_ep = emit_ep
    pend_ep()

    prev_mm = mm1_hist[NT - 1]
    t_ga = sp("vector", lambda e: e.tensor_mul(gsel[0:1, 0:D], gate(2),
                                               arow2),
              waits=[tok_sgo, tok_gates, tok_stash])
    tok_garep = replicate(garep, gsel[0:1, 0:D], [t_ga, tok_proj[NT - 1]])
    tok_proj_all = tok_proj[-1]
    if stop_phase == "P1":
        return _finish_program(nc, stack, P)
    dbg_guard = []
    if debug:
        dbg_guard.append(sp("gpsimd", lambda e: e.dma_start(
            out=DBG["proj"], in_=projbf.rearrange("p a b -> p (a b)")),
            waits=[tok_proj_all], dma_out=True))

    tok_qlkl = [None] * 4
    tok_gcol = [None] * 4
    prev_rot_v = tok_proj_all
    prev_rot_mm = prev_mm
    for g in range(4):
        qEv = projbf.rearrange("p t (gg c) -> p t gg c", gg=4)[
            :, :, g, 0:128].rearrange("p t (i u) -> p t i u", i=4)
        qOv = projbf.rearrange("p t (gg c) -> p t gg c", gg=4)[
            :, :, g, 128:256].rearrange("p t (i u) -> p t i u", i=4)
        br = lambda tab: tab[:, :, None, :].to_broadcast((128, NT, 4, 32))
        ro1v = ro1.rearrange("p t (i u) -> p t i u", i=4)
        ro2v = ro2.rearrange("p t (i u) -> p t i u", i=4)
        sc1 = scrC.rearrange("p (t c) -> p t c", t=NT)[
            :, :, 0:128].rearrange("p t (i u) -> p t i u", i=4)

        def rot_ops(E, O, C_, S_, first_wait):
            a1 = sp("vector", lambda e: e.tensor_tensor(
                out=ro1v, in0=E, in1=br(C_), op=ALU.mult), waits=first_wait)
            a2 = sp("vector", lambda e: e.tensor_tensor(
                out=sc1, in0=O, in1=br(S_), op=ALU.mult), waits=[a1])
            a3 = sp("vector", lambda e: e.tensor_tensor(
                out=ro1v, in0=ro1v, in1=sc1, op=ALU.subtract), waits=[a2])
            a4 = sp("vector", lambda e: e.tensor_tensor(
                out=ro2v, in0=E, in1=br(S_), op=ALU.mult), waits=[a3])
            a5 = sp("vector", lambda e: e.tensor_tensor(
                out=sc1, in0=O, in1=br(C_), op=ALU.mult), waits=[a4])
            a6 = sp("vector", lambda e: e.tensor_tensor(
                out=ro2v, in0=ro2v, in1=sc1, op=ALU.add), waits=[a5])
            return a6

        def transpose_stacks(dst1, dst2, rot_tok, mm_prev, ev_prev):
            tp_last, ev_last = mm_prev, ev_prev
            for (src, dst) in ((ro1, dst1), (ro2, dst2)):
                for tt in range(NT):
                    w = []
                    if tt % 4 == 0:
                        w = [rot_tok] + [ev_last if ev_last else tp_last]
                    tp_last = sp("tensor", lambda e, src=src, tt=tt:
                                 e.transpose(
                                     psumAb[:, 128 * (tt % 4):
                                            128 * (tt % 4) + 128],
                                     src[:, tt, :], idb), waits=w)
                    if tt % 4 == 3:
                        ev_last = sp("vector", lambda e, dst=dst, tt=tt:
                                     e.tensor_copy(
                                         out=dst[:, 128 * (tt - 3):
                                                 128 * (tt + 1)],
                                         in_=psumAb[:, 0:512]),
                                     waits=[tp_last])
            return tp_last, ev_last

        def u_mms(Ua, Ub, dst, ev_ready, first_extra):
            mm_u, uev_last = None, None
            for blk in range(4):
                for i in range(4):
                    w = []
                    if i == 0:
                        w = [uev_last] if uev_last else (
                            [ev_ready] + first_extra)
                    mm_u = sp("tensor", lambda e, i=i, blk=blk, Ua=Ua:
                              e.matmul(
                                  psumB[32 * i:32 * i + 32, 512:1024],
                                  Ua[32 * i:32 * i + 32, :],
                                  ts1[32 * i:32 * i + 32,
                                      512 * blk:512 * blk + 512],
                                  start=True, stop=False,
                                  skip_group_check=True,
                                  tile_position=(32 * i, 32 * i)), waits=w)
                for i in range(4):
                    mm_u = sp("tensor", lambda e, i=i, blk=blk, Ub=Ub:
                              e.matmul(
                                  psumB[32 * i:32 * i + 32, 512:1024],
                                  Ub[32 * i:32 * i + 32, :],
                                  ts2[32 * i:32 * i + 32,
                                      512 * blk:512 * blk + 512],
                                  start=False, stop=True,
                                  skip_group_check=True,
                                  tile_position=(32 * i, 32 * i)), waits=[])
                uev_last = sp("vector", lambda e, dst=dst, blk=blk:
                              e.tensor_copy(
                                  out=dst[:, 512 * blk:512 * blk + 512],
                                  in_=psumB[:, 512:1024]), waits=[mm_u])
            return mm_u, uev_last

        rtok_q0 = rot_ops(qEv, qOv, ctq, stq, [prev_rot_v])
        rtok_q = rtok_q0
        if debug and g == 0:
            sp("gpsimd", lambda e: e.dma_start(
                out=DBG["ro1"], in_=ro1.rearrange("p a b -> p (a b)")),
                waits=[rtok_q0], dma_out=True)
        tpq, evq = transpose_stacks(ts1, ts2,
                                    (P.sems["dOUT"], P.sems["dOUT"].count)
                                    if (debug and g == 0) else rtok_q0,
                                    prev_rot_mm, None)
        mmq, uevq = u_mms(u1qb, u2qb, qlS[g], evq, [tok_ucast])
        if debug and g == 0:
            uevq = sp("gpsimd", lambda e: e.dma_start(out=DBG["ts1"], in_=ts1),
                      waits=[uevq, evq], dma_out=True)
            uevq = (P.sems["dOUT"], P.sems["dOUT"].count)
        cp_last = uevq
        for a in range(4):
            for i in range(4):
                h = 4 * g + i
                srcE = projbf[:, h, 256 * a:256 * a + 128].rearrange(
                    "p (b u) -> p b u", b=4)
                srcO = projbf[:, h, 256 * a + 128:256 * a + 256].rearrange(
                    "p (b u) -> p b u", b=4)
                cp_last = sp("vector", lambda e, a=a, i=i, srcE=srcE:
                             e.tensor_copy(
                                 out=ket[:, 4 * a:4 * a + 4,
                                         32 * i:32 * i + 32], in_=srcE),
                             waits=[uevq] if (a == 0 and i == 0) else [])
                cp_last = sp("vector", lambda e, a=a, i=i, srcO=srcO:
                             e.tensor_copy(
                                 out=kot[:, 4 * a:4 * a + 4,
                                         32 * i:32 * i + 32], in_=srcO),
                             waits=[])
        kEv = ket.rearrange("p t (i u) -> p t i u", i=4)
        kOv = kot.rearrange("p t (i u) -> p t i u", i=4)
        rtok_k = rot_ops(kEv, kOv, ctk, stk, [cp_last])
        tpk, evk = transpose_stacks(ts1, ts2, rtok_k, mmq, None)
        mmk, uevk = u_mms(u1b, u2b, klS[g], evk, [])
        tok_qlkl[g] = uevk
        b1 = sp("vector", lambda e, g=g: e.tensor_mul(
            scrC, qlS[g], qlS[g]), waits=[uevk])
        mmn = None
        for blk in range(4):
            mmn = sp("tensor", lambda e, blk=blk: e.matmul(
                psumA[0:1, 512 * blk:512 * blk + 512], onescol,
                scrC[:, 512 * blk:512 * blk + 512], start=True, stop=True),
                waits=[b1, mmk] if blk == 0 else [])
        b2 = sp("vector", lambda e: e.reduce_max(
            out=C1(2), in_=psumA[0:1, 0:N], axis=AX.X), waits=[mmn])
        b3 = sp("vector", lambda e, g=g: e.tensor_mul(
            scrC, klS[g], klS[g]), waits=[b2])
        for blk in range(4):
            mmn = sp("tensor", lambda e, blk=blk: e.matmul(
                psumA[0:1, 512 * blk:512 * blk + 512], onescol,
                scrC[:, 512 * blk:512 * blk + 512], start=True, stop=True),
                waits=[b3] if blk == 0 else [])
        b4 = sp("vector", lambda e: e.reduce_max(
            out=C1(3), in_=psumA[0:1, 0:N], axis=AX.X), waits=[mmn])
        b5 = sp("vector", lambda e: e.tensor_mul(C1(4), C1(2), C1(3)),
                waits=[b4])
        b6 = sp("scalar", lambda e: e.activation(out=C1(5), in_=C1(4),
                                                 func=AF.Ln), waits=[b5])
        b7 = sp("scalar", lambda e: e.activation(out=C1(6), in_=C1(5),
                                                 func=AF.Exp, scale=0.5),
                waits=[b6])
        b8 = sp("vector", lambda e: e.tensor_scalar(
            out=C1(7), in0=C1(6), scalar1=-1.0, scalar2=None, op0=ALU.mult),
            waits=[b7])
        tok_gcol[g] = replicate_col(COL[f"gneg{g}"], C1(7), [b8])
        prev_rot_v = tok_gcol[g]
        prev_rot_mm = mmn

    if debug:
        dbg_guard.append(sp("gpsimd", lambda e: e.dma_start(
            out=DBG["ql0"], in_=qlS[0]), waits=[tok_qlkl[0]], dma_out=True))
        dbg_guard.append(sp("gpsimd", lambda e: e.dma_start(
            out=DBG["kl0"], in_=klS[0]), waits=[tok_qlkl[0]], dma_out=True))
        dbg_guard.append(sp("gpsimd", lambda e: e.dma_start(
            out=DBG["cols"], in_=cols), waits=[tok_gcol[3]], dma_out=True))

    tok_vset = sp("vector", lambda e: e.memset(vscr[:], 1.0),
                  waits=[tok_proj_all])
    vcp = tok_vset
    for h in range(H):
        for a in range(4):
            srcE = projbf[:, h, 256 * a:256 * a + 128].rearrange(
                "p (b u) -> p b u", b=4)
            srcO = projbf[:, h, 256 * a + 128:256 * a + 256].rearrange(
                "p (b u) -> p b u", b=4)
            vcp = sp("vector", lambda e, h=h, a=a, srcE=srcE: e.tensor_copy(
                out=vscr[:, 4 * a:4 * a + 4, h, 0:32], in_=srcE),
                waits=[tok_vset] if (h == 0 and a == 0) else [])
            vcp = sp("vector", lambda e, h=h, a=a, srcO=srcO: e.tensor_copy(
                out=vscr[:, 4 * a:4 * a + 4, h, 32:64], in_=srcO), waits=[])
    tok_vscr = vcp
    if stop_phase == "P3":
        return _finish_program(nc, stack, P)

    w2sum = cols[:, 37:38]
    w2sumt = cols[:, 38:39]
    p1b_state = {"v_last": None, "v_hist": []}
    p1b_queue = []

    def mk_p1b(w_hbm, kc, pc, wsum_col, wsumt_col, first):
        def emit():
            st = p1b_state
            if first:
                st["v_last"] = sp("vector",
                                  lambda e, c=wsum_col: e.memset(c, 0.0),
                                  waits=[tok_vscr])
            j = len(st["v_hist"])
            wdep = [st["v_last"]] if j < 2 else [st["v_hist"][j - 2]]
            dtok = sp("sync", lambda e, kc=kc, pc=pc, j=j: e.dma_start(
                out=wstrP[:, j % 2],
                in_=w_hbm[128 * kc:128 * kc + 128,
                          1024 * pc:1024 * pc + 1024]),
                waits=wdep, sem="dWa" if j % 2 == 0 else "dWb")
            vr = sp("vector", lambda e, j=j, c=wsumt_col: e.reduce_sum(
                out=c, in_=wstrP[:, j % 2], axis=AX.X,
                apply_absolute_value=True), waits=[dtok, st["v_last"]])
            va = sp("vector", lambda e, a=wsum_col, b=wsumt_col:
                    e.tensor_add(a, a, b), waits=[vr])
            st["v_last"] = va
            st["v_hist"].append(va)
        return emit

    for kc in range(8):
        for pc in range(4):
            p1b_queue.append(mk_p1b(w1T_in, kc, pc, COL["wsum"],
                                    COL["wsumt"], kc == 0 and pc == 0))
    for kc in range(32):
        p1b_queue.append(mk_p1b(w2T_in, kc, 0, w2sum, w2sumt, kc == 0))

    tok_x2 = [None] * NT
    prev_y_v = tok_vscr
    prev_ny = None
    exp_hist = []
    pvk_hist = []
    sc_hist = []
    tn_hist = []
    ny_prev_pair = None
    prev_wo_mm = prev_rot_mm
    ATT = [(qb, p, kc) for qb in range(4) for p in range(8)
           for kc in range(NT)]

    def att_sc(n):
        qb, p, kc = ATT[n]
        g = p // 2
        i0 = 2 * (p % 2)
        par = kc % 2
        w_sc = []
        if n >= 2:
            w_sc.append(exp_hist[n - 2])
        else:
            w_sc.append(tok_vscr)
        if kc == 0:
            w_sc += [tok_qlkl[g], tok_gcol[g]]
        mm_s = None
        for jj, i in enumerate((i0, i0 + 1)):
            mm_s = sp("tensor",
                      lambda e, g=g, i=i, jj=jj, kc=kc, qb=qb, par=par:
                      e.matmul(
                          psumA[:, 1024 * par + 512 * jj:
                                1024 * par + 512 * jj + 512],
                          klS[g][32 * i:32 * i + 32,
                                 128 * kc:128 * kc + 128],
                          qlS[g][32 * i:32 * i + 32,
                                 512 * qb:512 * qb + 512],
                          start=True, stop=True,
                          tile_position=(32 * i, 0)),
                      waits=w_sc if jj == 0 else [])
        sc_hist.append(mm_s)

    att_state = {"emitted": 0}

    def att_step(n):
        qb, p, kc = ATT[n]
        g = p // 2
        h0 = 4 * g + 2 * (p % 2)
        par = kc % 2
        while att_state["emitted"] <= min(n + 1, len(ATT) - 1):
            att_sc(att_state["emitted"])
            att_state["emitted"] += 1
        w_exp = [sc_hist[n]]
        if n >= 2:
            w_exp.append(pvk_hist[n - 2])
        tok_exp = sp("scalar", lambda e, g=g, par=par: e.activation(
            out=pbuf[:, par, 0:1024],
            in_=psumA[:, 1024 * par:1024 * par + 1024],
            func=AF.Exp, bias=COL[f"gneg{g}"], scale=1.0),
            waits=w_exp)
        exp_hist.append(tok_exp)
        mm_pv = None
        for jj, h in enumerate((h0, h0 + 1)):
            w_pv = [tok_exp]
            if kc == 0 and jj == 0:
                w_pv.append(prev_y_v)
                if n == 0:
                    w_pv.append(tok_vscr)
            mm_pv = sp("tensor",
                       lambda e, h=h, jj=jj, kc=kc, par=par:
                       e.matmul(
                           psumB[0:65, 512 * jj:512 * jj + 512],
                           vscr[:, kc, h, :],
                           pbuf[:, par, 512 * jj:512 * jj + 512],
                           start=(kc == 0), stop=(kc == NT - 1)),
                       waits=w_pv if jj == 0 else [])
        pvk_hist.append(mm_pv)
        return mm_pv

    for qb in range(4):
        tok_ydone = []
        pend_fin = None
        for p in range(8):
            g, j = p // 2, p % 2
            i0, i1 = 2 * j, 2 * j + 1
            h0, h1 = 4 * g + i0, 4 * g + i1
            pair_pv = None
            if pend_fin is not None:
                pend_fin()
                pend_fin = None
            for _ in range(2):
                if p1b_queue:
                    p1b_queue.pop(0)()
            for kc in range(NT):
                pair_pv = att_step((qb * 8 + p) * NT + kc)
            ev_y = []
            for jj, i in enumerate((i0, i1)):
                w_ev = [pair_pv]
                if jj == 0 and len(tn_hist) >= 2:
                    w_ev.append(tn_hist[-2])
                ev_y.append(sp("vector", lambda e, jj=jj, i=i: e.tensor_copy(
                    out=ystg[:, i, :],
                    in_=psumB[0:65, 512 * jj:512 * jj + 512]),
                    waits=w_ev))
            prev_y_v = ev_y[-1]

            def make_fin(p=p, ev_y=ev_y, i0=i0, i1=i1, h0=h0, h1=h1):
                def fin():
                    nonlocal ny_prev_pair, prev_ny
                    tn = []
                    for jj, i in enumerate((i0, i1)):
                        for sub in range(4):
                            w = []
                            if jj == 0 and sub == 0:
                                if ny_prev_pair is not None:
                                    w.append(ny_prev_pair)
                                if p == 0 and prev_ny is not None:
                                    w.append(prev_ny)
                            if sub == 0:
                                w = w + [ev_y[jj]]
                            tn.append(sp("tensor",
                                         lambda e, jj=jj, i=i, sub=sub:
                                         e.transpose(
                                psumB[:, 1024 + 512 * jj + 65 * sub:
                                      1024 + 512 * jj + 65 * sub + 65],
                                ystg[0:65, i, 128 * sub:128 * sub + 128],
                                idf[0:65, 0:65]), waits=w))
                    tn_hist.append(tn[-1])
                    ny = None
                    for jj, h in enumerate((h0, h1)):
                        pview = psumB[:, 1024 + 512 * jj:
                                      1024 + 512 * jj + 260].rearrange(
                            "p (s c) -> p s c", c=65)
                        w_rc = [tn[4 * jj + 3]] + ([ny] if ny else [])
                        rc = sp("vector", lambda e, pview=pview:
                                e.reciprocal(
                            out=rdcols, in_=pview[:, :, 64]), waits=w_rc)
                        ny = sp("vector", lambda e, h=h, pview=pview:
                                e.tensor_tensor(
                            out=ytok[:, :, h, :], in0=pview[:, :, 0:64],
                            in1=rdcols[:, :, None].to_broadcast(
                                (128, 4, 64)),
                            op=ALU.mult), waits=[rc])
                    ny_prev_pair = ny
                    prev_ny = ny
                    tok_ydone.append(ny)
                return fin

            pend_fin = make_fin()
        pend_fin()
        pend_fin = None
        for sub in range(4):
            tglob = 4 * qb + sub
            w_q = [tok_ydone[-1]]
            q1 = sp("vector", lambda e, sub=sub: e.reduce_max(
                out=COL["ma"], in_=ytok[:, sub].rearrange("p h c -> p (h c)"),
                axis=AX.X, apply_absolute_value=True), waits=w_q)
            q2 = sp("vector", lambda e: e.tensor_scalar(
                out=COL["ma"], in0=COL["ma"], scalar1=1e-6, scalar2=None,
                op0=ALU.max), waits=[q1])
            q3 = sp("vector", lambda e: e.reciprocal(out=COL["rma"],
                                                     in_=COL["ma"]),
                    waits=[q2])
            q4 = sp("vector", lambda e, sub=sub: e.tensor_scalar(
                out=may4[sub], in0=COL["ma"], scalar1=float(1.0 / 7.0),
                scalar2=None, op0=ALU.mult), waits=[q2])
            q5 = sp("vector", lambda e, sub=sub: e.tensor_scalar(
                out=scrA, in0=ytok[:, sub].rearrange("p h c -> p (h c)"),
                scalar1=COL["rma"], scalar2=7.0, op0=ALU.mult, op1=ALU.mult),
                waits=[q3])
            q6 = sp("vector", lambda e: e.tensor_scalar(
                out=rtok, in0=scrA, scalar1=MAGIC, scalar2=MAGIC,
                op0=ALU.add, op1=ALU.subtract), waits=[q5])
            tp = []
            for fc in range(8):
                w = [q6]
                if fc == 0:
                    w += [prev_ny, prev_wo_mm]
                tp.append(sp("tensor", lambda e, fc=fc: e.transpose(
                    psumBb[:, 2048 + 128 * fc:2048 + 128 * fc + 128],
                    rtok[:, 128 * fc:128 * fc + 128], idb), waits=w))
            ev = []
            for half in range(2):
                ev.append(sp("vector", lambda e, half=half, sub=sub:
                             e.tensor_copy(
                                 out=q4T.rearrange("p b k c -> p b (k c)")[
                                     :, sub % 2, 512 * half:512 * half + 512],
                                 in_=psumBb[:, 2048 + 512 * half:
                                            2048 + 512 * half + 512]),
                             waits=[tp[-1]]))
            prev_ny = ev[-1]
            mm = None
            for kc in range(8):
                for half in range(2):
                    mm = sp("tensor", lambda e, kc=kc, half=half, sub=sub:
                            e.matmul(
                                psumB[:, 512 * half:512 * half + 512],
                                q4T[:, sub % 2, kc, :],
                                sgo[:, kc, 512 * half:512 * half + 512],
                                start=(kc == 0), stop=(kc == 7)),
                            waits=[ev[0], ev[1], tok_sgo, prev_y_v]
                            if (kc == 0 and half == 0) else [])
            ep = []
            for half in range(2):
                e1 = sp("vector", lambda e, half=half, sub=sub:
                        e.scalar_tensor_tensor(
                            out=scrB[:, 512 * half:512 * half + 512],
                            in0=psumB[:, 512 * half:512 * half + 512],
                            scalar=may4[sub],
                            in1=garep[:, 512 * half:512 * half + 512],
                            op0=ALU.mult, op1=ALU.mult),
                        waits=[mm, q4, tok_garep])
                ep.append(sp("vector", lambda e, half=half, tglob=tglob:
                             e.tensor_add(
                                 xt[:, tglob, 512 * half:512 * half + 512],
                                 xt[:, tglob, 512 * half:512 * half + 512],
                                 scrB[:, 512 * half:512 * half + 512]),
                             waits=[e1]))
            tok_x2[tglob] = ep[-1]
            prev_y_v = ep[-1]
            prev_wo_mm = mm

    if stop_phase == "P4":
        return _finish_program(nc, stack, P)
    tok_dump_x2 = None
    if debug:
        tok_dump_x2 = sp("gpsimd", lambda e: e.dma_start(
            out=DBG["x2"], in_=xt.rearrange("p a b -> p (a b)")),
            waits=[tok_x2[NT - 1]], dma_out=True)

    tok_n1 = sp("sync", lambda e: e.dma_start(out=n1wrow, in_=n1w_in),
                waits=[tok_x2[NT - 1]], sem="dR")
    tok_n2 = sp("sync", lambda e: e.dma_start(out=n2wrow, in_=n2w_in),
                waits=[tok_x2[NT - 1]], sem="dR")
    tok_mr = sp("gpsimd", lambda e: e.dma_start(out=modrow2, in_=modstash),
                waits=[tok_stash,
                       (P.sems["sV"], tok_x2[NT - 1][1])])
    tok_dR = (P.sems["dR"], P.sems["dR"].count)
    tok_n1 = tok_n2 = tok_dR

    def gate2(i):
        return modrow2[0:1, D * i:D * (i + 1)]

    tok_cb1 = replicate(cb1rep, gate2(4),
                        [tok_mr, tok_x2[-1]] +
                        ([(P.sems["dOUT"], P.sems["dOUT"].count)]
                         if debug else []))
    tok_sb = replicate(sbrep, gate2(3), [tok_mr])
    tok_gb = replicate(gbrep, gate2(5), [tok_mr])
    tok_w1n = replicate(w1nrep, n1wrow, [tok_n1])
    tok_w2n = replicate(w2nrep, n2wrow, [tok_n2])

    def prep_binary(w_hbm, nkc, width, q_dst, mean_col, wsum_col,
                    tok_prev, mm_prev):
        pw = min(1024, width)
        npc = width // pw
        pieces = [(kc, pc) for kc in range(nkc) for pc in range(npc)]
        v_last = sp("vector", lambda e: e.nop(), waits=[tok_prev, p1b_state["v_last"]])
        tmm = sp("tensor", lambda e, c=wsum_col: e.transpose(
            psumA[0:1, 0:128], c, idf),
            waits=[v_last] + ([mm_prev] if mm_prev else []))
        e2 = sp("vector", lambda e: e.reduce_sum(
            out=C1(0), in_=psumA[0:1, 0:128], axis=AX.X), waits=[tmm])
        e3 = sp("vector", lambda e, nkc=nkc, width=width: e.tensor_scalar(
            out=C1(1), in0=C1(0), scalar1=float(1.0 / (nkc * 128 * width)),
            scalar2=None, op0=ALU.mult), waits=[e2])
        e4 = sp("vector", lambda e: e.tensor_scalar(
            out=C1(1), in0=C1(1), scalar1=EPS_Q, scalar2=None, op0=ALU.max),
            waits=[e3])
        e5 = sp("vector", lambda e: e.reciprocal(out=C1(2), in_=C1(1)),
                waits=[e4])
        tok_scale = replicate_col(COL["thr"], C1(2), [e5])
        tok_mean = replicate_col(mean_col, C1(1), [e5])
        q_last, q_hist, m1_hist = tok_scale, [], []
        for j, (kc, pc) in enumerate(pieces):
            wdep = [tok_mean] if j < 4 else [m1_hist[j - 4]]
            dtok = sp("sync", lambda e, kc=kc, pc=pc, j=j, pw=pw: e.dma_start(
                out=wstrW[:, j % 4, 0:pw],
                in_=w_hbm[128 * kc:128 * kc + 128, pw * pc:pw * pc + pw]),
                waits=wdep, sem=("dWa", "dWb", "dWc", "dWd")[j % 4])
            scr = wq_scr[:, pw * (j % 2):pw * (j % 2) + pw]
            m1w = [dtok, tok_scale]
            if j >= 2:
                m1w.append(q_hist[j - 2])
            m1 = sp("scalar", lambda e, j=j, pw=pw, scr=scr: e.activation(
                out=scr, in_=wstrW[:, j % 4, 0:pw], func=AF.Copy,
                bias=MAGIC, scale=COL["thr"]), waits=m1w)
            m1_hist.append(m1)
            m2 = sp("vector", lambda e, pw=pw, scr=scr: e.tensor_scalar(
                out=scr, in0=scr, scalar1=MAGIC,
                scalar2=1.0, op0=ALU.subtract, op1=ALU.min), waits=[m1])
            q_last = sp("vector", lambda e, kc=kc, pc=pc, pw=pw, scr=scr:
                        e.tensor_scalar(
                out=q_dst[:, kc, pw * pc:pw * pc + pw], in0=scr,
                scalar1=-1.0, scalar2=None, op0=ALU.max), waits=[m2])
            q_hist.append(q_last)
        return q_last

    tok_w1q = prep_binary(w1T_in, 8, HID, w1q, COL["s1g"], COL["wsum"],
                          tok_w2n, prev_wo_mm)
    tok_w2q = prep_binary(w2T_in, 32, D, w2q, COL["s2g"], w2sum,
                          tok_w1q, None)
    if stop_phase == "PREP2":
        return _finish_program(nc, stack, P)

    fA = fbufA.rearrange("p a b -> p (a b)")
    st5 = {"prev_ffn_mm": prev_wo_mm}
    ffn1_last = {}
    relu_hist = {}
    h2_hist = {}
    ev2_hist = {}
    epi_hist = {}
    ev_hist = {}
    A_ctx = {}
    B_ctx = {}
    out_toks = []

    def stageA(t):
        w0 = [tok_x2[t]]
        if t >= 1 and (t - 1) in A_ctx:
            w0.append(A_ctx[t - 1]["u1t"])
        xh, _sq1 = ln_stats(xt[:, t, :], w0)
        h2a = sp("vector", lambda e: e.tensor_mul(scrA, scrA, cb1rep),
                 waits=[xh, tok_cb1])
        h2b = sp("vector", lambda e: e.tensor_add(scrA, scrA, sbrep),
                 waits=[h2a, tok_sb])
        s2 = sp("scalar", lambda e, t=t: e.activation(
            out=rbuf[:, t % 2, 0:D], in_=scrA, func=AF.Square,
            accum_out=COL["v1"]), waits=[h2b])
        d1 = sp("vector", lambda e: e.tensor_scalar(
            out=COL["v1"], in0=COL["v1"], scalar1=1.0 / D, scalar2=EPS_RMS,
            op0=ALU.mult, op1=ALU.add), waits=[s2])
        d2 = sp("scalar", lambda e: e.activation(out=COL["lnt2"],
                                                 in_=COL["v1"], func=AF.Ln),
                waits=[d1])
        d3 = sp("scalar", lambda e: e.activation(out=COL["rs1"],
                                                 in_=COL["lnt2"],
                                                 func=AF.Exp, scale=-0.5),
                waits=[d2])
        u1t = sp("vector", lambda e: e.tensor_mul(scrB, scrA, w1nrep),
                 waits=[h2b, tok_w1n])
        m1 = sp("vector", lambda e: e.reduce_max(
            out=COL["mx"], in_=scrB, axis=AX.X, apply_absolute_value=True),
            waits=[u1t])
        m2 = sp("vector", lambda e: e.tensor_tensor(
            out=COL["mx"], in0=COL["mx"], in1=COL["rs1"], op=ALU.mult),
            waits=[m1, d3])
        m3 = sp("vector", lambda e: e.tensor_scalar(
            out=COL["mx"], in0=COL["mx"], scalar1=EPS_Q, scalar2=None,
            op0=ALU.max), waits=[m2])
        m4 = sp("vector", lambda e: e.reciprocal(out=COL["mxr"],
                                                 in_=COL["mx"]), waits=[m3])
        m5 = sp("vector", lambda e: e.tensor_scalar(
            out=COL["s127"], in0=COL["mxr"], scalar1=127.0, scalar2=None,
            op0=ALU.mult), waits=[m4])
        m6 = sp("vector", lambda e: e.tensor_tensor(
            out=COL["sc"], in0=COL["s127"], in1=COL["rs1"], op=ALU.mult),
            waits=[m5])
        cnp = cols[:, 64 + t % 3:65 + t % 3]
        m7 = sp("vector", lambda e, cnp=cnp: e.tensor_scalar(
            out=cnp, in0=COL["mx"], scalar1=float(1.0 / 127.0),
            scalar2=None, op0=ALU.mult), waits=[m3])
        m8 = sp("vector", lambda e, cnp=cnp: e.tensor_tensor(
            out=cnp, in0=cnp, in1=COL["s1g"], op=ALU.mult),
            waits=[m7, tok_w1q])
        g1 = sp("vector", lambda e: e.tensor_scalar(
            out=scrB, in0=scrB, scalar1=COL["sc"], scalar2=MAGIC,
            op0=ALU.mult, op1=ALU.add), waits=[m6])
        g2 = sp("vector", lambda e: e.tensor_scalar(
            out=rtok, in0=scrB, scalar1=MAGIC, scalar2=None,
            op0=ALU.subtract), waits=[g1])
        tp = []
        for fc in range(8):
            w = [g2]
            if fc == 0:
                w.append(st5["prev_ffn_mm"])
                if t >= 1:
                    w.append(ev_hist[t - 1])
                else:
                    w.append(exp_hist[-1])
            tp.append(sp("tensor", lambda e, fc=fc: e.transpose(
                psumAb[:, 128 * fc:128 * fc + 128],
                rtok[:, 128 * fc:128 * fc + 128], idb), waits=w))
        ev = []
        for half in range(2):
            ev.append(sp("vector", lambda e, half=half, t=t: e.tensor_copy(
                out=q4T.rearrange("p b k c -> p b (k c)")[
                    :, t % 3, 512 * half:512 * half + 512],
                in_=psumAb[:, 512 * half:512 * half + 512]), waits=[tp[-1]]))
        ev_hist[t] = ev[-1]
        A_ctx[t] = {"ev": ev, "m8": m8, "u1t": u1t}

    def stageA2(t):
        ev = A_ctx[t]["ev"]
        relu_toks = []
        mm = None
        for ob in range(8):
            for kc in range(8):
                w = []
                if kc == 0:
                    w = [ev[0], ev[1], tok_w1q]
                    if ob >= 2:
                        w.append(relu_toks[ob - 2])
                mm = sp("tensor", lambda e, kc=kc, ob=ob, t=t: e.matmul(
                    psumA[:, 1024 + 512 * (ob % 2):1536 + 512 * (ob % 2)],
                    q4T[:, t % 3, kc, :],
                    w1q[:, kc, 512 * ob:512 * ob + 512],
                    start=(kc == 0), stop=(kc == 7)), waits=w)
            relu_toks.append(sp("scalar", lambda e, ob=ob, t=t: e.activation(
                out=rbuf[:, t % 2, 512 * ob:512 * ob + 512],
                in_=psumA[:, 1024 + 512 * (ob % 2):1536 + 512 * (ob % 2)],
                func=AF.Relu, scale=cols[:, 64 + t % 3:65 + t % 3]),
                waits=[mm, A_ctx[t]["m8"]]))
        ffn1_last[t] = mm
        relu_hist[t] = relu_toks[-1]

    def stageB(t):
        w_sq = [relu_hist[t]]
        if t >= 1:
            w_sq.append(h2_hist[t - 1])
        sq = sp("scalar", lambda e, t=t: e.activation(
            out=fA, in_=rbuf[:, t % 2], func=AF.Square), waits=w_sq)
        ha = sp("vector", lambda e: e.tensor_scalar(
            out=fA, in0=fA, scalar1=COL["asc"], scalar2=COL["abi"],
            op0=ALU.mult, op1=ALU.add), waits=[sq, tok_asc, tok_abi])
        s3 = sp("scalar", lambda e: e.activation(
            out=r2tok, in_=fA, func=AF.Square, accum_out=COL["v2"]),
            waits=[ha])
        u2t = sp("vector", lambda e, t=t: e.tensor_mul(
            rbuf[:, t % 2], fA, w2nrep), waits=[ha, tok_w2n])
        n1 = sp("vector", lambda e, t=t: e.reduce_max(
            out=COL["mx2"], in_=rbuf[:, t % 2], axis=AX.X,
            apply_absolute_value=True), waits=[u2t])
        c0 = sp("vector", lambda e: e.tensor_scalar(
            out=COL["v2"], in0=COL["v2"], scalar1=1.0 / HID, scalar2=EPS_RMS,
            op0=ALU.mult, op1=ALU.add), waits=[s3])
        c1 = sp("scalar", lambda e: e.activation(out=COL["lnt3"],
                                                 in_=COL["v2"], func=AF.Ln),
                waits=[c0])
        c2 = sp("scalar", lambda e: e.activation(out=COL["sqv"],
                                                 in_=COL["lnt3"],
                                                 func=AF.Exp, scale=0.5),
                waits=[c1])
        c3 = sp("scalar", lambda e: e.activation(out=COL["rs2"],
                                                 in_=COL["lnt3"],
                                                 func=AF.Exp, scale=-0.5),
                waits=[c2])
        c4 = sp("vector", lambda e: e.tensor_scalar(
            out=COL["sqv"], in0=COL["sqv"], scalar1=EPS_Q, scalar2=None,
            op0=ALU.mult), waits=[c2])
        n3 = sp("vector", lambda e: e.tensor_max(
            COL["mx2"], COL["mx2"], COL["sqv"]), waits=[n1, c4])
        n4 = sp("vector", lambda e: e.reciprocal(out=COL["mx2r"],
                                                 in_=COL["mx2"]), waits=[n3])
        n5 = sp("vector", lambda e: e.tensor_scalar(
            out=COL["sc2"], in0=COL["mx2r"], scalar1=127.0, scalar2=None,
            op0=ALU.mult), waits=[n4])
        n7 = sp("vector", lambda e: e.tensor_scalar(
            out=COL["cn2"], in0=COL["mx2"], scalar1=float(1.0 / 127.0),
            scalar2=None, op0=ALU.mult), waits=[n3])
        n7b = sp("vector", lambda e: e.tensor_tensor(
            out=COL["cn2"], in0=COL["cn2"], in1=COL["rs2"], op=ALU.mult),
            waits=[n7, c3])
        n8 = sp("vector", lambda e: e.tensor_tensor(
            out=COL["cn2"], in0=COL["cn2"], in1=COL["s2g"], op=ALU.mult),
            waits=[n7b, tok_w2q])
        h1_ = sp("vector", lambda e, t=t: e.tensor_scalar(
            out=fA, in0=rbuf[:, t % 2], scalar1=COL["sc2"], scalar2=MAGIC,
            op0=ALU.mult, op1=ALU.add), waits=[n5, s3])
        h2_w = [h1_]
        if t >= 1:
            h2_w.append(ev2_hist[t - 1])
        h2_ = sp("vector", lambda e: e.tensor_scalar(
            out=r2tok, in0=fA, scalar1=MAGIC, scalar2=None,
            op0=ALU.subtract), waits=h2_w)
        h2_hist[t] = h2_
        B_ctx[t] = {"n8": n8, "h2q": [h2_] * 4}

    def stageB2(t):
        n8 = B_ctx[t]["n8"]
        mm = None
        ev2_last = None
        h2q = B_ctx[t]["h2q"]
        for kg in range(8):
            tpg = []
            for fc in range(4):
                fcg = 4 * kg + fc
                w = []
                if fc == 0:
                    w = [h2q[kg // 2]] if kg == 0 else [mm]
                    if kg == 0:
                        w.append(st5["prev_ffn_mm"])
                tpg.append(sp("tensor", lambda e, fcg=fcg, fc=fc:
                              e.transpose(
                                  psumAb[:, 1024 + 128 * fc:
                                         1024 + 128 * fc + 128],
                                  r2tok[:, 128 * fcg:128 * fcg + 128], idb),
                              waits=w))
            ev2_last = sp("vector", lambda e, kg=kg: e.tensor_copy(
                out=r2Tm.rearrange("p b k c -> p b (k c)")[:, kg % 2, :],
                in_=psumAb[:, 1024:1536]), waits=[tpg[-1]])
            for kcl in range(4):
                for ob in range(2):
                    w = []
                    if kcl == 0 and ob == 0:
                        w = [ev2_last, tok_w2q]
                        if kg == 0:
                            w.append(epi_hist[t - 2] if t >= 2
                                     else tok_x2[NT - 1])
                    mm = sp("tensor", lambda e, kg=kg, kcl=kcl, ob=ob, t=t:
                            e.matmul(
                                psumB[:, 1024 * (t % 2) + 512 * ob:
                                      1024 * (t % 2) + 512 * ob + 512],
                                r2Tm[:, kg % 2, kcl, :],
                                w2q[:, 4 * kg + kcl, 512 * ob:512 * ob + 512],
                                start=(kg == 0 and kcl == 0),
                                stop=(kg == 7 and kcl == 3)), waits=w)
        ev2_hist[t] = ev2_last
        ep_toks = []
        for ob in range(2):
            e1 = sp("vector", lambda e, ob=ob, t=t: e.scalar_tensor_tensor(
                out=scrB[:, 512 * ob:512 * ob + 512],
                in0=psumB[:, 1024 * (t % 2) + 512 * ob:
                          1024 * (t % 2) + 512 * ob + 512],
                scalar=COL["cn2"],
                in1=gbrep[:, 512 * ob:512 * ob + 512],
                op0=ALU.mult, op1=ALU.mult), waits=[mm, n8, tok_gb])
            ep_toks.append(sp("vector", lambda e, ob=ob, t=t: e.tensor_add(
                xt[:, t, 512 * ob:512 * ob + 512],
                xt[:, t, 512 * ob:512 * ob + 512],
                scrB[:, 512 * ob:512 * ob + 512]), waits=[e1]))
        st5["prev_ffn_mm"] = mm
        epi_hist[t] = ep_toks[-1]
        out_toks.append(sp("gpsimd", lambda e, t=t: e.dma_start(
            out=out_hbm.rearrange("(t p) d -> p t d", p=128)[:, t, :],
            in_=xt[:, t, :]), waits=[ep_toks[-1]], dma_out=True))

    stageA(0)
    stageA(1)
    for t in range(NT):
        if t + 2 < NT:
            stageA(t + 2)
        if t >= 1:
            stageB(t - 1)
        stageA2(t)
        if t >= 1:
            stageB2(t - 1)
    stageB(NT - 1)
    stageB2(NT - 1)

    sp("gpsimd", lambda e: e.nop(),
       waits=[(P.sems["dOUT"], P.sems["dOUT"].count)], noinc=True)

    return _finish_program(nc, stack, P)


def _finish_program(nc, stack, P):
    P.mark_needed()

    with nc.Block() as block:
        @block.tensor
        def _(e):
            P.replay("tensor", e)

        @block.vector
        def _(e):
            P.replay("vector", e)

        @block.scalar
        def _(e):
            P.replay("scalar", e)

        @block.sync
        def _(e):
            P.replay("sync", e)

        @block.gpsimd
        def _(e):
            P.replay("gpsimd", e)

    return nc, stack


def kernel(L0_attn_x, L0_attn_Wqkv_w, L0_attn_Wqkv_b, L0_attn_U, L0_attn_Wo_w,
           L0_ffn_norm1_w, L0_ffn_w1, L0_ffn_act_scale, L0_ffn_act_bias,
           L0_ffn_norm2_w, L0_ffn_w2, L0_c, L0_wmod):
    f32 = np.float32
    perm = sigma3_perm()
    yp = y_perm()
    ct, st_, ctk_, stk_ = rotary_tables()

    U = np.asarray(L0_attn_U, f32)
    shared = dict(
        wqkvT=np.ascontiguousarray(np.asarray(L0_attn_Wqkv_w, f32).T[:, perm]),
        bq=np.ascontiguousarray(np.asarray(L0_attn_Wqkv_b, f32)[perm])[None, :],
        woT=np.ascontiguousarray(np.asarray(L0_attn_Wo_w, f32).T[yp, :]),
        w1T=np.ascontiguousarray(np.asarray(L0_ffn_w1, f32).T),
        w2T=np.ascontiguousarray(np.asarray(L0_ffn_w2, f32).T),
        u1rep=np.ascontiguousarray(np.tile(U[0:32], (4, 1))),
        u2rep=np.ascontiguousarray(np.tile(U[32:64], (4, 1))),
        wmod=np.ascontiguousarray(np.asarray(L0_wmod, f32)),
        n1w=np.asarray(L0_ffn_norm1_w, f32)[None, :],
        n2w=np.asarray(L0_ffn_norm2_w, f32)[None, :],
        asc=np.asarray(L0_ffn_act_scale, f32).reshape(1, 1),
        abi=np.asarray(L0_ffn_act_bias, f32).reshape(1, 1),
        idf=np.eye(128, dtype=f32),
        idb=np.eye(128, dtype=f32).astype(ml_dtypes.bfloat16),
        ctq=ct, stq=st_, ctk=ctk_, stk=stk_,
        onesrow=np.ones((1, 128), f32),
        onescol=np.ones((128, 1), f32).astype(ml_dtypes.bfloat16),
    )

    x = np.asarray(L0_attn_x, f32)
    c = np.asarray(L0_c, f32)
    in_maps = []
    for b in range(8):
        m = dict(shared)
        m["x"] = np.ascontiguousarray(x[b])
        m["c"] = np.ascontiguousarray(c[b])[None, :]
        in_maps.append(m)

    nc, stack = build_program()
    try:
        res = run_bass_kernel_spmd(nc, in_maps, core_ids=list(range(8)))
    finally:
        stack.close()
    return np.stack([np.asarray(res.results[b]["out"]) for b in range(8)])


